# version 05
# baseline (speedup 1.0000x reference)
import numpy as np
import concourse.bass as bass
import concourse.mybir as mybir
from concourse.bass_utils import run_bass_kernel_spmd

F32 = mybir.dt.float32
BF16 = mybir.dt.bfloat16
Act = mybir.ActivationFunctionType
Alu = mybir.AluOpType
AX = mybir.AxisListType

D = 1024
S = 8192
NKC = 16
NG = 8
NSLOT = 16
NEG_BIG = -3.0e38

X1_O = 0
STATS_O = 32768
IDB_O = 33024
S1S_O = 35328
S3B_O = 35840
QFM_O = 39936
WKBF_O = 72704
WVBF_O = 89088
XS_O = (105472, 121856)
XSB_O = (138240, 146432)
KST_O = (154624, 162816)
VST_O = (171008, 179200)
IDF_O = 187392
QSTG_O = 188416
KJ_O = (72704, 89088)
VJ_O = (105472, 121856)
PSB_O = (138240, 140288, 142336)
PTSB_O = (144384, 146432)
SMSK_O = (148480, 152576)
OACC_O = 156672
XQST_O = (164864, 168960)
TFIN_O = (173056, 177152)
CT_O = (181248, 185344)
DMASK_O = 189440
X1FM_O = 39936
OSB_O = (39936, 134144)
AFM_O = 56320
BFM_O = 121856
WC_O = 187392
WCB_O = (195584, 199680)
S1T_O = (121856, 123904)
HBUF_O = (187392, 189440)
HBUFB_O = (191488, 192512)
W3B_O = (187392, 191488)
W3BB_O = (195584, 197632)
CT2_O = (125952, 130048)
ARENA_BYTES = 204800

NEGM_C = 0
L_C = 16
NMT_C = 32
AD_C = 35
ALPHA_C = 38
RS_C = 41
LINV_C = 44
MUS_C = 46
MU_C = 48
VS_C = 50
RV_C = 52
LNV_C = 54
RSTD_C = 56

DMA_SEMS = ("SM", "XS0", "XS1", "Q", "GK0", "GK1", "GV0", "GV1",
            "KJ0", "KJ1", "VJ0", "VJ1", "XQ0", "XQ1",
            "WC", "HB0", "HB1", "W30", "W31", "OUT")
ALL_SEMS = ("T", "V", "SC") + DMA_SEMS


class KB:

    def __init__(self, debug=False):
        self.debug = debug
        nc = bass.Bass("TRN2", target_bir_lowering=False)
        self.nc = nc
        dram = nc.dram_tensor
        self.in_aps = {}
        def din(name, shape, dtype=F32):
            self.in_aps[name] = dram(name, shape, dtype, kind="ExternalInput").ap()
            return self.in_aps[name]
        self.xkv = din("xkv", [NKC, 128, 8, 512])
        self.xqfm = din("xqfm", [128, 8, 2048])
        self.xqtok = din("xqtok", [NSLOT, 128, 1024])
        self.wkT = din("wkT", [128, 8, 1024])
        self.wvT = din("wvT", [128, 8, 1024])
        self.wblk = din("wblk", [32, 128, 2, 8, 128])
        self.h1t = din("h1t", [8, 32, 128, 512])
        self.h2t = din("h2t", [8, 32, 128, 512])
        self.w3t = din("w3t", [32, 128, 1024])
        self.s1l = din("s1l", [128, 32])
        self.s2l = din("s2l", [128, 32])
        self.sh1l = din("sh1l", [128, 32])
        self.sh2l = din("sh2l", [128, 32])
        self.s3bl = din("s3bl", [128, 1024])
        self.identf = din("identf", [128, 128])
        self.dmaskin = din("dmaskin", [2, 128, 1024])
        self.outd = dram("out", [NSLOT, 128, 1024], F32, kind="ExternalOutput").ap()
        self.kscr = dram("kscr", [NG, 128, 8, 1024], BF16, kind="Internal").ap()
        self.vscr = dram("vscr", [NG, 128, 8, 1024], BF16, kind="Internal").ap()
        if debug:
            self.dbg_x1 = dram("dbg_x1", [128, 16, 1024], BF16,
                               kind="ExternalOutput").ap()
            self.dbg_st = dram("dbg_st", [128, 64], F32,
                               kind="ExternalOutput").ap()
            self.dbg_k0 = dram("dbg_k0", [128, 8, 1024], BF16,
                               kind="ExternalOutput").ap()
            self.dbg_v0 = dram("dbg_v0", [128, 8, 1024], BF16,
                               kind="ExternalOutput").ap()
            self.dbg_qfm = dram("dbg_qfm", [128, 8, 2048], BF16,
                                kind="ExternalOutput").ap()

        self.ps = nc.alloc_psum_tensor("ps", [128, 4096], F32).ap()
        nc.psum_base = 7
        self.psbf = nc.alloc_psum_tensor("psbf", [128, 1024], BF16).ap()

        arena = nc.alloc_sbuf_tensor("arena", [128, ARENA_BYTES], mybir.dt.uint8)
        base = nc.lookup_mloc(arena).addr
        def at(name, off, shape, dtype):
            return nc.alloc_sbuf_tensor_at(name, shape, dtype, offset=base + off).ap()
        self.x1 = at("x1", X1_O, [128, 16, 1024], BF16)
        self.st = at("stats", STATS_O, [128, 64], F32)
        self.idb = at("idb", IDB_O, [128, 128], BF16)
        self.dmask = at("dmask", DMASK_O, [128, 2, 1024], F32)
        self.s1s = at("s1s", S1S_O, [128, 32], F32)
        self.s2s = at("s2s", S1S_O + 128, [128, 32], F32)
        self.sh1s = at("sh1s", S1S_O + 256, [128, 32], F32)
        self.sh2s = at("sh2s", S1S_O + 384, [128, 32], F32)
        self.s3b = at("s3b", S3B_O, [128, 1024], F32)
        self.qfm = at("qfm", QFM_O, [128, 8, 2048], BF16)
        self.wkbf = at("wkbf", WKBF_O, [128, 8, 1024], BF16)
        self.wvbf = at("wvbf", WVBF_O, [128, 8, 1024], BF16)
        self.xs = [at(f"xs{i}", XS_O[i], [128, 8, 512], F32) for i in range(2)]
        self.xsb = [at(f"xsb{i}", XSB_O[i], [128, 8, 512], BF16) for i in range(2)]
        self.kst = [at(f"kst{i}", KST_O[i], [128, 8, 512], BF16) for i in range(2)]
        self.vst = [at(f"vst{i}", VST_O[i], [128, 4, 1024], BF16) for i in range(2)]
        self.idf = at("idf", IDF_O, [128, 128], F32)
        self.qstg = at("qstg", QSTG_O, [128, 2, 2048], F32)
        self.kj = [at(f"kj{i}", KJ_O[i], [128, 8, 1024], BF16) for i in range(2)]
        self.vj = [at(f"vj{i}", VJ_O[i], [128, 8, 1024], BF16) for i in range(2)]
        self.psb = [at(f"psb{i}", PSB_O[i], [128, 1024], BF16) for i in range(3)]
        self.ptsb = [at(f"ptsb{i}", PTSB_O[i], [128, 1024], BF16) for i in range(2)]
        self.xqst = [at(f"xqst{i}", XQST_O[i], [128, 1024], F32) for i in range(2)]
        self.tfin = [at(f"tfin{i}", TFIN_O[i], [128, 1024], F32) for i in range(2)]
        self.ct = [at(f"ct{i}", CT_O[i], [128, 1024], F32) for i in range(2)]
        self.smsk = [at(f"smsk{i}", SMSK_O[i], [128, 1024], F32) for i in range(2)]
        self.oacc = at("oacc", OACC_O, [128, 2, 1024], F32)
        self.x1fm = at("x1fm", X1FM_O, [128, 8, 1024], BF16)
        self.osb = [at(f"osb{i}", OSB_O[i], [128, 4, 1024], F32)
                    for i in range(2)]
        self.afm = at("afm", AFM_O, [128, 32, 1024], BF16)
        self.bfm = at("bfm", BFM_O, [128, 32, 1024], BF16)
        self.wc = at("wc", WC_O, [128, 2, 8, 128], F32)
        self.wcb = [at(f"wcb{i}", WCB_O[i], [128, 2, 8, 128], BF16) for i in range(2)]
        self.s1t = [at(f"s1t{i}", S1T_O[i], [128, 512], F32) for i in range(2)]
        self.hbuf = [at(f"hbuf{i}", HBUF_O[i], [128, 512], F32) for i in range(2)]
        self.hbufb = [at(f"hbufb{i}", HBUFB_O[i], [128, 512], BF16) for i in range(2)]
        self.w3b = [at(f"w3b{i}", W3B_O[i], [128, 1024], F32) for i in range(2)]
        self.w3bb = [at(f"w3bb{i}", W3BB_O[i], [128, 1024], BF16) for i in range(2)]
        self.ct2 = [at(f"ct2{i}", CT2_O[i], [128, 1024], F32) for i in range(2)]

        self.sems = {n: nc.alloc_semaphore(name=n, num=160 + i)
                     for i, n in enumerate(ALL_SEMS)}
        self.cnt = {n: 0 for n in self.sems}
        self.steps = []

    def emit(self, eng, fn):
        self.steps.append((eng, fn))

    def wait(self, eng, sem, val):
        if val <= 0:
            return
        h = self.sems[sem]
        self.emit(eng, lambda e, h=h, v=val: e.wait_ge(h, v))

    def op(self, eng, fn, sem=None, n=1):
        if eng == "vector":
            self.wait(eng, "V", self.cnt["V"])
        elif eng == "scalar":
            self.wait(eng, "SC", self.cnt["SC"])
        if sem is None:
            self.emit(eng, lambda e, fn=fn: fn(e))
            return None
        self.cnt[sem] += n
        h = self.sems[sem]
        self.emit(eng, lambda e, fn=fn, h=h, n=n: fn(e).then_inc(h, n))
        return self.cnt[sem]

    def dma(self, eng, out, in_, sem):
        return self.op(eng, lambda e, o=out, i=in_: e.dma_start(out=o, in_=i),
                       sem, 16)

    def barrier_waits(self, eng, marks):
        for s, v in marks.items():
            self.wait(eng, s, v)

    def snap(self):
        return dict(self.cnt)

    def phase1(self):
        st, ps = self.st, self.ps
        for out, in_ in ((self.idf[:], self.identf),
                         (self.s1s[:], self.s1l), (self.s2s[:], self.s2l),
                         (self.sh1s[:], self.sh1l), (self.sh2s[:], self.sh2l),
                         (self.s3b[:], self.s3bl)):
            self.wait("sync", "SM", self.cnt["SM"])
            c_small = self.dma("sync", out, in_, "SM")
        self.c_small = c_small
        v_wk = [0, 0, 0, 0]
        for w, (src, dstbf) in enumerate(((self.wkT, self.wkbf),
                                          (self.wvT, self.wvbf))):
            for h in range(2):
                stg = self.xs[h]
                stgv = stg.rearrange("p a f -> p (a f)").rearrange(
                    "p (a f) -> p a f", a=4)
                if w == 1:
                    self.wait("sync", "V", v_wk[h])
                cl = self.dma("sync", stgv, src[:, 4 * h:4 * h + 4, :],
                              f"XS{h}")
                self.wait("vector", f"XS{h}", cl)
                cv = self.op("vector", lambda e, s=stgv, d=dstbf, h=h:
                             e.tensor_copy(out=d[:, 4 * h:4 * h + 4, :], in_=s),
                             "V")
                v_wk[w * 2 + h] = cv
        self.wait("vector", "SM", c_small)
        self.v_id = self.op("vector", lambda e: e.tensor_copy(
            out=self.idb[:], in_=self.idf[:]), "V")

        t_xsb_used = [0] * NKC
        v_cast = [0] * NKC
        gp_k = [0] * NKC
        gp_v = [0] * NKC
        in_x = [0] * NKC
        v_kevac = {}
        s_vevac = {}
        qf_done = 0

        for kc in range(NKC):
            slot = kc % 2
            if kc >= 2:
                self.wait("sync", "V", v_cast[kc - 2])
            if kc < 2:
                self.wait("sync", "V", v_wk[2 + kc])
            in_x[kc] = self.dma("sync", self.xs[slot][:], self.xkv[kc],
                                f"XS{slot}")
            if kc in (1, 3, 5, 7):
                q = (kc - 1) // 2
                cq = self.dma("sync", self.qstg[:],
                              self.xqfm[:, 2 * q:2 * q + 2, :], "Q")
                self.wait("vector", "Q", cq)
                qf_done = self.op(
                    "vector", lambda e, q=q:
                    e.tensor_copy(out=self.qfm[:, 2 * q:2 * q + 2, :],
                                  in_=self.qstg[:]), "V")
                self.wait("sync", "V", qf_done)

            self.wait("vector", f"XS{slot}", in_x[kc])
            if kc >= 2:
                self.wait("vector", "T", t_xsb_used[kc - 2])
            v_cast[kc] = self.op(
                "vector", lambda e, slot=slot:
                e.tensor_copy(out=self.xsb[slot][:], in_=self.xs[slot][:]),
                "V")

            self.wait("tensor", "V", v_cast[kc])
            for od in range(8):
                bank = od % 2
                if (kc, od - 2) in v_kevac:
                    self.wait("tensor", "V", v_kevac[(kc, od - 2)])
                elif kc > 0 and od < 2:
                    self.wait("tensor", "V", v_kevac[(kc - 1, od + 6)])
                for dk in range(8):
                    self.op("tensor", lambda e, od=od, dk=dk, slot=slot, bank=bank:
                            e.matmul(self.ps[:, bank * 512:(bank + 1) * 512],
                                     self.wkbf[:, dk, od * 128:(od + 1) * 128],
                                     self.xsb[slot][:, dk, :],
                                     start=(dk == 0), stop=(dk == 7)),
                            "T" if dk == 7 else None)
                tmark = self.cnt["T"]
                self.wait("vector", "T", tmark)
                if kc >= 2 and od == 0:
                    self.wait("vector", f"GK{slot}", gp_k[kc - 2])
                v_kevac[(kc, od)] = self.op(
                    "vector", lambda e, slot=slot, od=od, bank=bank:
                    e.tensor_copy(out=self.kst[slot][:, od, :],
                                  in_=self.ps[:, bank * 512:(bank + 1) * 512]),
                    "V")
            for i in range(8):
                ts, oh = divmod(i, 2)
                bank = 2 + i % 2
                if (kc, i - 2) in s_vevac:
                    self.wait("tensor", "SC", s_vevac[(kc, i - 2)])
                elif kc > 0 and i < 2:
                    self.wait("tensor", "SC", s_vevac[(kc - 1, i + 6)])
                for dk in range(8):
                    self.op("tensor", lambda e, ts=ts, oh=oh, dk=dk, slot=slot, bank=bank:
                            e.matmul(self.ps[:, bank * 512:(bank + 1) * 512],
                                     self.xsb[slot][:, dk, ts * 128:(ts + 1) * 128],
                                     self.wvbf[:, dk, oh * 512:(oh + 1) * 512],
                                     start=(dk == 0), stop=(dk == 7)),
                            "T" if dk == 7 else None)
                tmark = self.cnt["T"]
                self.wait("scalar", "T", tmark)
                if kc >= 2 and i == 0:
                    self.wait("scalar", f"GV{slot}", gp_v[kc - 2])
                s_vevac[(kc, i)] = self.op(
                    "scalar", lambda e, slot=slot, ts=ts, oh=oh, bank=bank:
                    e.copy(out=self.vst[slot][:, ts, oh * 512:(oh + 1) * 512],
                           in_=self.ps[:, bank * 512:(bank + 1) * 512]),
                    "SC")
            t_xsb_used[kc] = self.cnt["T"]
            jg, jh = divmod(kc, 2)
            self.wait("gpsimd", "V", v_kevac[(kc, 7)])
            gp_k[kc] = self.dma("gpsimd",
                                self.kscr[jg][:, :, jh * 512:(jh + 1) * 512],
                                self.kst[slot][:], f"GK{slot}")
            self.wait("gpsimd", "SC", s_vevac[(kc, 7)])
            gp_v[kc] = self.dma("gpsimd",
                                self.vscr[jg][:, jh * 4:(jh + 1) * 4, :],
                                self.vst[slot][:], f"GV{slot}")

        self.qf_done = qf_done

    def phase2(self, p1_marks):
        st, ps = self.st, self.ps
        nc = self.nc
        from concourse.hw_specs import get_activation_tables
        tabs = list(get_activation_tables(nc.m.arch))
        nle = tabs.index("natural_log_exp_and_others")
        def load_tab(e, idx=nle):
            return e.add_instruction(mybir.InstLoadActFuncSet(
                name=self.nc.get_next_instruction_name(),
                act_func_set_id=idx, ins=[], outs=[]))
        self.op("scalar", load_tab)

        for eng in ("sync", "gpsimd", "tensor", "vector", "scalar"):
            self.barrier_waits(eng, p1_marks)
        self.wait("sync", "SM", self.cnt["SM"])
        dm_ld = self.dma("sync", self.dmask[:],
                         self.dmaskin.rearrange("v p f -> p v f"), "SM")
        self.dm_ld = dm_ld

        items = [(g, jg) for g in range(NG) for jg in range(g + 1)]
        n_items = len(items)
        ld_k = [0] * n_items
        ld_v = [0] * n_items
        t_av_done = [0] * n_items
        issued = [0]

        def issue_load(i):
            g, jg = items[i]
            buf = i % 2
            if i >= 2:
                assert t_av_done[i - 2] > 0
                self.wait("sync", "T", t_av_done[i - 2])
                self.wait("gpsimd", "T", t_av_done[i - 2])
            ld_k[i] = self.dma("sync", self.kj[buf][:], self.kscr[jg],
                               f"KJ{buf}")
            ld_v[i] = self.dma("gpsimd", self.vj[buf][:], self.vscr[jg],
                               f"VJ{buf}")

        pair = 0
        s_exp = {}
        s_exp_m = {}
        t_tr = {}
        s_pt = {}
        t_av = {}
        v_lup = {}
        v_oacc = {}
        oacc_free = [0, 0]
        bank_rel = {}
        av_rel = {}
        smsk_busy = [0, 0]
        x1ready = [0] * NSLOT
        gp_xq = [0] * NSLOT
        v_xq_used = [0, 0]

        def finalize(m, par, g):
            self.op("vector", lambda e, m=m, par=par:
                    e.reciprocal(out=st[:, LINV_C + par:LINV_C + par + 1],
                                 in_=st[:, L_C + m:L_C + m + 1]), "V")
            vmark = self.cnt["V"]
            self.wait("scalar", "V", max(vmark, v_oacc[m]))
            c = self.op("scalar", lambda e, par=par:
                        e.activation(out=self.tfin[par][:],
                                     in_=self.oacc[:, par, :],
                                     func=Act.Copy,
                                     scale=st[:, LINV_C + par:LINV_C + par + 1]),
                        "SC")
            oacc_free[par] = c
            self.wait("vector", "SC", c)
            self.wait("vector", f"XQ{par}", gp_xq[m])
            v_xq_used[par] = self.op(
                "vector", lambda e, par=par:
                e.scalar_tensor_tensor(out=self.tfin[par][:],
                                       in0=self.xqst[par][:], scalar=2.0,
                                       in1=self.tfin[par][:],
                                       op0=Alu.mult, op1=Alu.add), "V")
            self.op("vector", lambda e, par=par:
                    e.reduce_sum(out=st[:, MUS_C + par:MUS_C + par + 1],
                                 in_=self.tfin[par][:], axis=AX.X), "V")
            self.op("vector", lambda e, par=par:
                    e.tensor_scalar_mul(out=st[:, MU_C + par:MU_C + par + 1],
                                        in0=st[:, MUS_C + par:MUS_C + par + 1],
                                        scalar1=1.0 / 1024.0), "V")
            self.op("vector", lambda e, par=par:
                    e.tensor_scalar_sub(out=self.ct[par][:],
                                        in0=self.tfin[par][:],
                                        scalar1=st[:, MU_C + par:MU_C + par + 1]),
                    "V")
            self.op("vector", lambda e, par=par:
                    e.scalar_tensor_tensor(out=self.tfin[par][:],
                                           in0=self.ct[par][:], scalar=1.0,
                                           in1=self.ct[par][:],
                                           op0=Alu.mult, op1=Alu.mult,
                                           accum_out=st[:, VS_C + par:VS_C + par + 1]),
                    "V")
            vmark = self.op("vector", lambda e, par=par:
                            e.tensor_scalar(out=st[:, RV_C + par:RV_C + par + 1],
                                            in0=st[:, VS_C + par:VS_C + par + 1],
                                            scalar1=1.0 / 1024.0, scalar2=1e-5,
                                            op0=Alu.mult, op1=Alu.add), "V")
            self.wait("scalar", "V", vmark)
            self.op("scalar", lambda e, par=par:
                    e.activation(out=st[:, LNV_C + par:LNV_C + par + 1],
                                 in_=st[:, RV_C + par:RV_C + par + 1],
                                 func=Act.Ln), "SC")
            smark = self.op("scalar", lambda e, par=par:
                            e.activation(out=st[:, RSTD_C + par:RSTD_C + par + 1],
                                         in_=st[:, LNV_C + par:LNV_C + par + 1],
                                         func=Act.Exp, scale=-0.5), "SC")
            self.wait("vector", "SC", smark)
            x1ready[m] = self.op(
                "vector", lambda e, m=m, par=par:
                e.tensor_scalar_mul(out=self.x1[:, m, :], in0=self.ct[par][:],
                                    scalar1=st[:, RSTD_C + par:RSTD_C + par + 1]),
                "V")

        pend = []

        def emit_tail(ctx):
            p = ctx["pair"]
            m, mi, buf = ctx["m"], ctx["mi"], ctx["buf"]
            p2, p3 = ctx["p2"], ctx["p3"]
            i, first, diag, g = ctx["i"], ctx["first"], ctx["diag"], ctx["g"]
            self.wait("tensor", "SC", s_exp[p])
            if p >= 1:
                self.wait("tensor", "SC", s_pt[p - 1])
            for s4 in range(8):
                self.op("tensor", lambda e, p3=p3, s4=s4:
                        e.transpose(
                            self.psbf[:, s4 * 128:(s4 + 1) * 128],
                            self.psb[p3][:, s4 * 128:(s4 + 1) * 128],
                            self.idb[:]),
                        "T" if s4 == 7 else None)
            t_tr[p] = self.cnt["T"]
            self.wait("scalar", "T", t_tr[p])
            if p >= 2:
                self.wait("scalar", "T", t_av[p - 2])
            s_pt[p] = self.op(
                "scalar", lambda e, p2=p2:
                e.copy(out=self.ptsb[p2][:], in_=self.psbf[:]), "SC")
            self.wait("tensor", "SC", s_pt[p])
            if p >= 1:
                self.wait("tensor", "V", av_rel[p - 1])
            assert ld_v[i] > 0, (i, p)
            self.wait("tensor", f"VJ{buf}", ld_v[i])
            ab = 4
            for ks in range(8):
                for oh in range(2):
                    self.op("tensor",
                            lambda e, ks=ks, oh=oh, ab=ab, p2=p2, buf=buf:
                            e.matmul(
                                ps[:, (ab + oh) * 512:(ab + oh + 1) * 512],
                                self.ptsb[p2][:, ks * 128:(ks + 1) * 128],
                                self.vj[buf][:, ks, oh * 512:(oh + 1) * 512],
                                start=(ks == 0), stop=(ks == 7)),
                            "T" if (ks == 7 and oh == 1) else None)
            t_av[p] = self.cnt["T"]
            t_av_done[i] = self.cnt["T"]
            self.wait("vector", "T", t_av[p])
            if getattr(self, "_tap_out", 0):
                self.wait("vector", "OUT", self._tap_out)
            if first:
                self.wait("vector", "SC", oacc_free[mi])
                av_rel[p] = self.op(
                    "vector", lambda e, mi=mi, ab=ab:
                    e.tensor_copy(out=self.oacc[:, mi, :],
                                  in_=ps[:, ab * 512:(ab + 2) * 512]),
                    "V")
            else:
                self.wait("vector", "SC", s_exp[p])
                av_rel[p] = self.op(
                    "vector", lambda e, mi=mi, ab=ab, p3=p3:
                    e.scalar_tensor_tensor(
                        out=self.oacc[:, mi, :],
                        in0=self.oacc[:, mi, :],
                        scalar=st[:, ALPHA_C + p3:ALPHA_C + p3 + 1],
                        in1=ps[:, ab * 512:(ab + 2) * 512],
                        op0=Alu.mult, op1=Alu.add), "V")
            v_oacc[m] = av_rel[p]
            if self.debug and p == getattr(self, "dbg_pair", -1):
                ds = getattr(self, "dbg_slot", 0)
                self.wait("gpsimd", "V", self.cnt["V"])
                self.wait("gpsimd", "SC", self.cnt["SC"])
                self.wait("gpsimd", "OUT", self.cnt["OUT"])
                self.dma("gpsimd", self.dbg_x1[:, 0:2, :].rearrange(
                    "p a f -> p (a f)").rearrange("p (a f) -> p a f", a=2),
                    self.oacc[:], "OUT")
                self._tap_out = self.cnt["OUT"]
                self.wait("gpsimd", "OUT", self.cnt["OUT"])
                self.op("gpsimd", lambda e, ds=ds: e.dma_start(
                    out=self.dbg_st[:, 0:1], in_=st[:, ds:ds + 1],
                    allow_slow_non_contiguous=True), "OUT", 16)
                self.wait("gpsimd", "OUT", self.cnt["OUT"])
                self.op("gpsimd", lambda e, ds=ds: e.dma_start(
                    out=self.dbg_st[:, 1:2], in_=st[:, L_C + ds:L_C + ds + 1],
                    allow_slow_non_contiguous=True), "OUT", 16)
                self._tap_out = self.cnt["OUT"]
            if ctx["last"]:
                nxt = i + 2
                if nxt < n_items and issued[0] <= nxt:
                    issue_load(nxt)
                    issued[0] = nxt + 1
            if diag:
                finalize(m, mi, g)

        item_i = 0
        for k0 in range(2):
            issue_load(k0)
        issued[0] = 2
        for g in range(NG):
            for jg in range(g + 1):
                i = item_i
                buf = i % 2
                for mi in range(2):
                    m = 2 * g + mi
                    p2 = pair % 2
                    p3 = pair % 3
                    diag = (jg == g)
                    first = (jg == 0)
                    sbank = 2 * p2
                    assert ld_k[i] > 0, (i, pair)
                    self.wait("tensor", f"KJ{buf}", ld_k[i])
                    if pair >= 2:
                        rsem, rval = bank_rel[pair - 2]
                        self.wait("tensor", rsem, rval)
                    if g == 0 and pair == 0:
                        self.wait("tensor", "V", self.qf_done)
                    for dk in range(8):
                        for kh in range(2):
                            self.op("tensor",
                                    lambda e, dk=dk, kh=kh, m=m, buf=buf, sbank=sbank:
                                    e.matmul(
                                        ps[:, (sbank + kh) * 512:(sbank + kh + 1) * 512],
                                        self.qfm[:, dk, m * 128:(m + 1) * 128],
                                        self.kj[buf][:, dk, kh * 512:(kh + 1) * 512],
                                        start=(dk == 0), stop=(dk == 7)),
                                    "T" if (dk == 7 and kh == 1) else None)
                    t_sc = self.cnt["T"]
                    self.wait("vector", "T", t_sc)
                    if getattr(self, "_tap_out", 0):
                        self.wait("vector", "OUT", self._tap_out)
                    if not first:
                        if pair >= 3:
                            self.wait("vector", "SC", s_exp[pair - 3])
                        self.wait("vector", "SC", s_exp_m.get(m, 0))
                    sc_src = ps[:, sbank * 512:(sbank + 2) * 512]
                    if diag:
                        self.wait("vector", "SC", smsk_busy[mi])
                        self.wait("vector", "SM", dm_ld)
                        mk = self.op("vector", lambda e, sbank=sbank, mi=mi:
                                     e.tensor_add(out=self.smsk[mi][:],
                                                  in0=ps[:, sbank * 512:(sbank + 2) * 512],
                                                  in1=self.dmask[:, mi, :]), "V")
                        bank_rel[pair] = ("V", mk)
                        sc_src = self.smsk[mi][:]
                    if first:
                        vmark = self.op(
                            "vector", lambda e, m=m, sc_src=sc_src:
                            e.reduce_max(out=st[:, NEGM_C + m:NEGM_C + m + 1],
                                         in_=sc_src,
                                         axis=AX.X, negate=True), "V")
                    else:
                        self.op("vector", lambda e, p3=p3, sc_src=sc_src:
                                e.reduce_max(out=st[:, NMT_C + p3:NMT_C + p3 + 1],
                                             in_=sc_src,
                                             axis=AX.X, negate=True), "V")
                        self.op("vector", lambda e, p3=p3, m=m:
                                e.tensor_scalar(out=st[:, AD_C + p3:AD_C + p3 + 1],
                                                in0=st[:, NMT_C + p3:NMT_C + p3 + 1],
                                                scalar1=st[:, NEGM_C + m:NEGM_C + m + 1],
                                                scalar2=0.0,
                                                op0=Alu.subtract, op1=Alu.min), "V")
                        vmark = self.op(
                            "vector", lambda e, p3=p3, m=m:
                            e.tensor_scalar(out=st[:, NEGM_C + m:NEGM_C + m + 1],
                                            in0=st[:, AD_C + p3:AD_C + p3 + 1],
                                            scalar1=st[:, NEGM_C + m:NEGM_C + m + 1],
                                            scalar2=None,
                                            op0=Alu.add), "V")
                    self.wait("scalar", "V", vmark)
                    if pair >= 3:
                        self.wait("scalar", "T", t_tr[pair - 3])
                    if not first:
                        self.op("scalar", lambda e, p3=p3:
                                e.activation(out=st[:, ALPHA_C + p3:ALPHA_C + p3 + 1],
                                             in_=st[:, AD_C + p3:AD_C + p3 + 1],
                                             func=Act.Exp), "SC")
                    bias_c = NEGM_C + m
                    acc = st[:, L_C + m:L_C + m + 1] if first else \
                        st[:, RS_C + p3:RS_C + p3 + 1]
                    s_exp[pair] = self.op(
                        "scalar", lambda e, p3=p3, sc_src=sc_src, bias_c=bias_c, acc=acc:
                        e.activation(out=self.psb[p3][:],
                                     in_=sc_src,
                                     func=Act.Exp,
                                     bias=st[:, bias_c:bias_c + 1],
                                     accum_out=acc), "SC")
                    if diag:
                        smsk_busy[mi] = s_exp[pair]
                    else:
                        bank_rel[pair] = ("SC", s_exp[pair])
                    s_exp_m[m] = s_exp[pair]
                    if not first:
                        self.wait("vector", "SC", s_exp[pair])
                        v_lup[pair] = self.op(
                            "vector", lambda e, m=m, p3=p3:
                            e.scalar_tensor_tensor(
                                out=st[:, L_C + m:L_C + m + 1],
                                in0=st[:, L_C + m:L_C + m + 1],
                                scalar=st[:, ALPHA_C + p3:ALPHA_C + p3 + 1],
                                in1=st[:, RS_C + p3:RS_C + p3 + 1],
                                op0=Alu.mult, op1=Alu.add), "V")
                    if len(pend) == 2:
                        emit_tail(pend.pop(0))
                    pend.append(dict(pair=pair, m=m, mi=mi, p2=p2, p3=p3,
                                     buf=buf, i=i, first=first, diag=diag,
                                     g=g, last=(mi == 1)))
                    pair += 1
                    if jg == 0 and mi == 1:
                        for mi2 in range(2):
                            m2 = 2 * g + mi2
                            self.wait("gpsimd", "V", v_xq_used[mi2])
                            gp_xq[m2] = self.dma("gpsimd", self.xqst[mi2][:],
                                                 self.xqtok[m2], f"XQ{mi2}")
                item_i += 1
        while pend:
            emit_tail(pend.pop(0))
        self.x1ready = x1ready
        if self.debug:
            self.wait("gpsimd", "T", self.cnt["T"])
            self.wait("gpsimd", "V", self.cnt["V"])
            self.wait("gpsimd", "OUT", self.cnt["OUT"])
            self.dma("gpsimd", self.dbg_qfm, self.qfm[:], "OUT")

    def phase3(self, p2_marks):
        st, ps = self.st, self.ps
        for eng in ("sync", "gpsimd", "tensor", "vector", "scalar"):
            self.barrier_waits(eng, p2_marks)

        from concourse.hw_specs import get_activation_tables
        tabs = list(get_activation_tables(self.nc.m.arch))
        def load_tab_f(idx):
            def f(e, idx=idx):
                return e.add_instruction(mybir.InstLoadActFuncSet(
                    name=self.nc.get_next_instruction_name(),
                    act_func_set_id=idx, ins=[], outs=[]))
            return f
        silu_i = tabs.index("silu_and_others")
        gelu_i = tabs.index("gelu_and_others")
        nle_i = tabs.index("natural_log_exp_and_others")

        gp_out = {}
        wguard = [0, 0]
        prev_s4_vadd = 0
        for t in range(2):
            sb = 8 * t
            v_tfe = [0] * 8
            for lm in range(8):
                m = sb + lm
                if lm >= 1:
                    self.wait("tensor", "V", v_tfe[lm - 1])
                if t == 1 and lm == 0:
                    self.wait("tensor", "OUT", gp_out[(0, 0, 3)])
                    self.wait("tensor", "V", prev_s4_vadd)
                for dd in range(8):
                    self.op("tensor", lambda e, m=m, dd=dd:
                            e.transpose(self.psbf[:, dd * 128:(dd + 1) * 128],
                                        self.x1[:, m, dd * 128:(dd + 1) * 128],
                                        self.idb[:]),
                            "T" if dd == 7 else None)
                tmark = self.cnt["T"]
                self.wait("vector", "T", tmark)
                v_tfe[lm] = self.op(
                    "vector", lambda e, lm=lm:
                    e.tensor_copy(
                        out=self.x1fm[:, :, lm * 128:(lm + 1) * 128],
                        in_=self.psbf.rearrange("p (a f) -> p a f", a=8)),
                    "V")
            self.op("scalar", load_tab_f(silu_i))
            v_wcast = [0] * 32
            t_mm1 = {}
            s_silu = {}
            v_astt = {}
            for c in range(32):
                if c >= 1:
                    self.wait("sync", "V", v_wcast[c - 1])
                else:
                    self.wait("sync", "V", wguard[0])
                    self.wait("sync", "T", wguard[1])
                cl = self.dma("sync", self.wc[:], self.wblk[c], "WC")
                self.wait("vector", "WC", cl)
                if c >= 2:
                    self.wait("vector", "T", t_mm1[(c - 2, 1)])
                elif wguard[1]:
                    self.wait("vector", "T", wguard[1])
                v_wcast[c] = self.op(
                    "vector", lambda e, c=c:
                    e.tensor_copy(out=self.wcb[c % 2][:], in_=self.wc[:]), "V")
                for th in range(2):
                    idx = c * 2 + th
                    b0 = (idx % 3) * 2
                    self.wait("tensor", "V", v_wcast[c])
                    if c == 0 and th == 0:
                        self.wait("tensor", "V", v_tfe[7])
                    if idx >= 3:
                        self.wait("tensor", "SC", s_silu[idx - 3])
                        self.wait("tensor", "V", v_astt[idx - 3])
                    for w in range(2):
                        for dd in range(8):
                            bank = b0 + w
                            self.op("tensor",
                                    lambda e, c=c, th=th, w=w, dd=dd, bank=bank:
                                    e.matmul(
                                        ps[:, bank * 512:(bank + 1) * 512],
                                        self.wcb[c % 2][:, w, dd, :],
                                        self.x1fm[:, dd, th * 512:(th + 1) * 512],
                                        start=(dd == 0), stop=(dd == 7)),
                                    "T" if (w == 1 and dd == 7) else None)
                    t_mm1[(c, th)] = self.cnt["T"]
                    self.wait("scalar", "T", t_mm1[(c, th)])
                    if idx >= 2:
                        self.wait("scalar", "V", v_astt[idx - 2])
                    s_silu[idx] = self.op(
                        "scalar", lambda e, c=c, idx=idx, b0=b0:
                        e.activation(out=self.s1t[idx % 2][:],
                                     in_=ps[:, b0 * 512:(b0 + 1) * 512],
                                     func=Act.Silu,
                                     scale=self.s1s[:, c:c + 1]), "SC")
                    self.wait("vector", "SC", s_silu[idx])
                    v_astt[idx] = self.op(
                        "vector", lambda e, c=c, th=th, idx=idx, b0=b0:
                        e.scalar_tensor_tensor(
                            out=self.afm[:, c, th * 512:(th + 1) * 512],
                            in0=ps[:, (b0 + 1) * 512:(b0 + 2) * 512],
                            scalar=self.s2s[:, c:c + 1],
                            in1=self.s1t[idx % 2][:],
                            op0=Alu.mult, op1=Alu.mult), "V")
            self.op("scalar", load_tab_f(gelu_i))
            if t == 1:
                self.wait("scalar", "OUT", gp_out[(0, 1, 3)])
            wguard = [v_wcast[31], t_mm1[(31, 1)]]
            for stage, (wt, src, dst, scl) in enumerate(
                    ((self.h1t, self.afm, self.bfm, self.sh1s),
                     (self.h2t, self.bfm, self.afm, self.sh2s))):
                v_hcast = {}
                t_hmm = {}
                s_gel = {}
                for sc in range(8):
                    for dd in range(32):
                        gi = sc * 32 + dd
                        slot = gi % 2
                        if gi >= 2:
                            self.wait("sync", "V", v_hcast[gi - 2])
                        else:
                            self.wait("sync", "V", wguard[0])
                            self.wait("sync", "T", wguard[1])
                        cl = self.dma("sync", self.hbuf[slot][:],
                                      wt[sc, dd], f"HB{slot}")
                        self.wait("vector", f"HB{slot}", cl)
                        if gi >= 2:
                            self.wait("vector", "T", t_hmm[gi - 2])
                        else:
                            self.wait("vector", "T", wguard[1])
                        v_hcast[gi] = self.op(
                            "vector", lambda e, slot=slot:
                            e.tensor_copy(out=self.hbufb[slot][:],
                                          in_=self.hbuf[slot][:]), "V")
                        self.wait("tensor", "V", v_hcast[gi])
                        if dd == 0 and sc == 0:
                            if stage == 0:
                                self.wait("tensor", "SC", s_silu[63])
                                self.wait("tensor", "V", v_astt[63])
                            else:
                                for c4 in range(4):
                                    self.wait("tensor", "SC", s_gel_prev[(7, c4)])
                        for c4 in range(4):
                            if dd == 0 and sc >= 1:
                                self.wait("tensor", "SC", s_gel[(sc - 1, c4)])
                            for th in range(2):
                                bank = c4 * 2 + th
                                self.op("tensor",
                                        lambda e, slot=slot, c4=c4, th=th, dd=dd,
                                        bank=bank, src=src:
                                        e.matmul(
                                            ps[:, bank * 512:(bank + 1) * 512],
                                            self.hbufb[slot][:, c4 * 128:(c4 + 1) * 128],
                                            src[:, dd, th * 512:(th + 1) * 512],
                                            start=(dd == 0), stop=(dd == 31)),
                                        "T" if (c4 == 3 and th == 1) else None)
                        t_hmm[gi] = self.cnt["T"]
                    for c4 in range(4):
                        self.wait("scalar", "T", t_hmm[sc * 32 + 31])
                        s_gel[(sc, c4)] = self.op(
                            "scalar", lambda e, sc=sc, c4=c4, dst=dst, scl=scl:
                            e.activation(out=dst[:, sc * 4 + c4, :],
                                         in_=ps[:, c4 * 1024:(c4 + 1) * 1024],
                                         func=Act.Gelu,
                                         scale=scl[:, sc * 4 + c4:sc * 4 + c4 + 1]),
                            "SC")
                s_gel_prev = s_gel
                wguard = [v_hcast[255], t_hmm[255]]
            self.op("scalar", load_tab_f(nle_i))
            v_add = {}
            v_w3cast = {}
            t_w3mm = {}
            for h in range(2):
                for ddb in range(4):
                    for dl in range(8):
                        dd = ddb * 8 + dl
                        gi = h * 32 + dd
                        slot = gi % 2
                        if gi >= 2:
                            self.wait("sync", "V", v_w3cast[gi - 2])
                        else:
                            self.wait("sync", "V", wguard[0])
                            self.wait("sync", "T", wguard[1])
                        cl = self.dma("sync", self.w3b[slot][:],
                                      self.w3t[dd], f"W3{slot}")
                        self.wait("vector", f"W3{slot}", cl)
                        if gi >= 2:
                            self.wait("vector", "T", t_w3mm[gi - 2])
                        else:
                            self.wait("vector", "T", wguard[1])
                        v_w3cast[gi] = self.op(
                            "vector", lambda e, slot=slot:
                            e.tensor_copy(out=self.w3bb[slot][:],
                                          in_=self.w3b[slot][:]), "V")
                        self.wait("tensor", "V", v_w3cast[gi])
                        if dl == 0:
                            if (h, ddb) == (0, 0):
                                for c4 in range(4):
                                    self.wait("tensor", "SC", s_gel_prev[(7, c4)])
                            else:
                                for tc in range(4):
                                    self.wait("tensor", "V",
                                              v_add[(h, ddb - 1, tc) if ddb > 0
                                                    else (h - 1, 3, tc)])
                        for tc in range(4):
                            for oh in range(2):
                                bank = tc * 2 + oh
                                self.op("tensor",
                                        lambda e, slot=slot, tc=tc, oh=oh, dd=dd,
                                        h=h, bank=bank, dl=dl:
                                        e.matmul(
                                            ps[:, bank * 512:(bank + 1) * 512],
                                            self.afm[:, dd, h * 512 + tc * 128:
                                                     h * 512 + (tc + 1) * 128],
                                            self.w3bb[slot][:, oh * 512:(oh + 1) * 512],
                                            start=(dl == 0), stop=(dl == 7)),
                                        "T" if (tc == 3 and oh == 1) else None)
                        t_w3mm[gi] = self.cnt["T"]
                    ob = self.osb[h]
                    for tc in range(4):
                        self.wait("vector", "T", t_w3mm[h * 32 + ddb * 8 + 7])
                        if ddb == 0:
                            if t == 1:
                                self.wait("vector", "OUT", gp_out[(0, h, 3)])
                            v_add[(h, ddb, tc)] = self.op(
                                "vector", lambda e, tc=tc, ob=ob:
                                e.tensor_copy(out=ob[:, tc, :],
                                              in_=ps[:, tc * 1024:(tc + 1) * 1024]),
                                "V")
                        else:
                            v_add[(h, ddb, tc)] = self.op(
                                "vector", lambda e, tc=tc, ob=ob:
                                e.tensor_add(out=ob[:, tc, :],
                                             in0=ob[:, tc, :],
                                             in1=ps[:, tc * 1024:(tc + 1) * 1024]),
                                "V")
                ob = self.osb[h]
                for tc in range(4):
                    m = sb + h * 4 + tc
                    par = tc % 2
                    self.op("vector", lambda e, tc=tc, ob=ob:
                            e.tensor_mul(out=ob[:, tc, :],
                                         in0=ob[:, tc, :],
                                         in1=self.s3b[:]), "V")
                    self.op("vector", lambda e, tc=tc, m=m, par=par, ob=ob:
                            e.scalar_tensor_tensor(out=self.ct2[par][:],
                                                   in0=self.x1[:, m, :],
                                                   scalar=1.0,
                                                   in1=ob[:, tc, :],
                                                   op0=Alu.mult, op1=Alu.add),
                            "V")
                    self.op("vector", lambda e, par=par:
                            e.reduce_sum(out=st[:, MUS_C + par:MUS_C + par + 1],
                                         in_=self.ct2[par][:], axis=AX.X), "V")
                    self.op("vector", lambda e, par=par:
                            e.tensor_scalar_mul(out=st[:, MU_C + par:MU_C + par + 1],
                                                in0=st[:, MUS_C + par:MUS_C + par + 1],
                                                scalar1=1.0 / 1024.0), "V")
                    self.op("vector", lambda e, par=par:
                            e.tensor_scalar_sub(out=self.ct2[par][:],
                                                in0=self.ct2[par][:],
                                                scalar1=st[:, MU_C + par:MU_C + par + 1]),
                            "V")
                    self.op("vector", lambda e, par=par, tc=tc, ob=ob:
                            e.scalar_tensor_tensor(out=ob[:, tc, :],
                                                   in0=self.ct2[par][:], scalar=1.0,
                                                   in1=self.ct2[par][:],
                                                   op0=Alu.mult, op1=Alu.mult,
                                                   accum_out=st[:, VS_C + par:VS_C + par + 1]),
                            "V")
                    vmark = self.op(
                        "vector", lambda e, par=par:
                        e.tensor_scalar(out=st[:, RV_C + par:RV_C + par + 1],
                                        in0=st[:, VS_C + par:VS_C + par + 1],
                                        scalar1=1.0 / 1024.0, scalar2=1e-5,
                                        op0=Alu.mult, op1=Alu.add), "V")
                    self.wait("scalar", "V", vmark)
                    self.op("scalar", lambda e, par=par:
                            e.activation(out=st[:, LNV_C + par:LNV_C + par + 1],
                                         in_=st[:, RV_C + par:RV_C + par + 1],
                                         func=Act.Ln), "SC")
                    smark = self.op(
                        "scalar", lambda e, par=par:
                        e.activation(out=st[:, RSTD_C + par:RSTD_C + par + 1],
                                     in_=st[:, LNV_C + par:LNV_C + par + 1],
                                     func=Act.Exp, scale=-0.5), "SC")
                    self.wait("vector", "SC", smark)
                    vmark = self.op(
                        "vector", lambda e, tc=tc, par=par, ob=ob:
                        e.tensor_scalar_mul(out=ob[:, tc, :],
                                            in0=self.ct2[par][:],
                                            scalar1=st[:, RSTD_C + par:RSTD_C + par + 1]),
                        "V")
                    self.wait("gpsimd", "V", vmark)
                    self.wait("gpsimd", "OUT", self.cnt["OUT"])
                    gp_out[(t, h, tc)] = self.dma(
                        "gpsimd", self.outd[m], ob[:, tc, :], "OUT")
            wguard = [v_w3cast[63], t_w3mm[63]]
            prev_s4_vadd = v_add[(1, 3, 3)]
        if self.debug:
            self.wait("gpsimd", "V", self.cnt["V"])
            self.wait("gpsimd", "SC", self.cnt["SC"])
            self.wait("gpsimd", "OUT", self.cnt["OUT"])
            if getattr(self, "dbg_pair", -1) < 0:
                self.dma("gpsimd", self.dbg_x1, self.x1[:], "OUT")
            self.wait("gpsimd", "OUT", self.cnt["OUT"])
            if getattr(self, "dbg_pair", -1) < 0:
                self.dma("gpsimd", self.dbg_st[:, 0:32], self.st[:, 0:32], "OUT")
            self.wait("gpsimd", "OUT", self.cnt["OUT"])
            m1 = self.dma("gpsimd", self.kj[0][:], self.kscr[0], "OUT")
            self.wait("gpsimd", "OUT", m1)
            self.dma("gpsimd", self.dbg_k0, self.kj[0][:], "OUT")
            self.wait("gpsimd", "OUT", self.cnt["OUT"])
            m2 = self.dma("gpsimd", self.vj[0][:], self.vscr[0], "OUT")
            self.wait("gpsimd", "OUT", m2)
            self.dma("gpsimd", self.dbg_v0, self.vj[0][:], "OUT")
        self.wait("gpsimd", "OUT", self.cnt["OUT"])

    def build(self):
        self.phase1()
        p1 = self.snap()
        self.phase2(p1)
        p2 = self.snap()
        self.phase3(p2)
        steps = self.steps
        with self.nc.Block() as block:
            @block.sync
            def _(e):
                for eng, fn in steps:
                    if eng == "sync":
                        fn(e)

            @block.gpsimd
            def _(e):
                for eng, fn in steps:
                    if eng == "gpsimd":
                        fn(e)

            @block.tensor
            def _(e):
                for eng, fn in steps:
                    if eng == "tensor":
                        fn(e)

            @block.vector
            def _(e):
                for eng, fn in steps:
                    if eng == "vector":
                        fn(e)

            @block.scalar
            def _(e):
                for eng, fn in steps:
                    if eng == "scalar":
                        fn(e)
        return self.nc


def _stage_inputs(inp):
    f32 = np.float32
    x = np.ascontiguousarray(inp["L0_attn_x"], dtype=f32)
    W = np.ascontiguousarray(inp["L0_attn_W"], dtype=f32)
    wkT = np.ascontiguousarray(W[:D].T)
    wvT = np.ascontiguousarray(W[D:].T)
    wk_l = np.ascontiguousarray(wkT.reshape(8, 128, 1024).transpose(1, 0, 2))
    wv_l = np.ascontiguousarray(wvT.reshape(8, 128, 1024).transpose(1, 0, 2))
    w1 = np.asarray(inp["L0_ffn_w1_w"], dtype=f32)
    w2 = np.asarray(inp["L0_ffn_w2_w"], dtype=f32)
    wblk = np.ascontiguousarray(
        np.stack([w1.reshape(32, 128, 8, 128), w2.reshape(32, 128, 8, 128)],
                 axis=1).transpose(0, 4, 1, 3, 2))
    h1 = np.asarray(inp["L0_ffn_h1_w"], dtype=f32)
    h2 = np.asarray(inp["L0_ffn_h2_w"], dtype=f32)
    h1t = np.ascontiguousarray(h1.reshape(8, 512, 32, 128).transpose(0, 2, 3, 1))
    h2t = np.ascontiguousarray(h2.reshape(8, 512, 32, 128).transpose(0, 2, 3, 1))
    w3 = np.asarray(inp["L0_ffn_w3_w"], dtype=f32)
    w3t = np.ascontiguousarray(w3.T.reshape(32, 128, 1024))
    s1l = np.ascontiguousarray(np.asarray(inp["L0_ffn_w1_s"], f32).reshape(32, 128).T)
    s2l = np.ascontiguousarray(np.asarray(inp["L0_ffn_w2_s"], f32).reshape(32, 128).T)
    sh1l = np.ascontiguousarray(np.asarray(inp["L0_ffn_h1_s"], f32).reshape(32, 128).T)
    sh2l = np.ascontiguousarray(np.asarray(inp["L0_ffn_h2_s"], f32).reshape(32, 128).T)
    s3bl = np.ascontiguousarray(
        np.broadcast_to(np.asarray(inp["L0_ffn_w3_s"], f32)[None, :], (128, 1024)))
    identf = np.eye(128, dtype=f32)

    in_maps = []
    for core in range(8):
        b, c = divmod(core, 4)
        xb = x[b]
        qrows = ((4 * np.arange(16) + c)[:, None] * 128 +
                 np.arange(128)[None, :]).reshape(-1)
        xq = np.ascontiguousarray(xb[qrows])
        xqfm = np.ascontiguousarray(
            xq.T.reshape(8, 128, 2048).transpose(1, 0, 2))
        xqtok = np.ascontiguousarray(xq.reshape(16, 128, 1024))
        xkv = np.ascontiguousarray(
            xb.reshape(16, 512, 8, 128).transpose(0, 3, 2, 1))
        i = np.arange(128)[:, None]
        kk = np.arange(1024)[None, :]
        dmsk = np.stack([
            np.where(kk <= v * 512 + c * 128 + i, np.float32(0.0),
                     np.float32(NEG_BIG)).astype(f32)
            for v in range(2)])
        in_maps.append(dict(
            xkv=xkv, xqfm=xqfm, xqtok=xqtok, wkT=wk_l, wvT=wv_l,
            wblk=wblk, h1t=h1t, h2t=h2t, w3t=w3t,
            s1l=s1l, s2l=s2l, sh1l=sh1l, sh2l=sh2l, s3bl=s3bl,
            identf=identf, dmaskin=dmsk))
    return in_maps


_NC_CACHE = {}


def _get_nc():
    if "nc" not in _NC_CACHE:
        _NC_CACHE["nc"] = KB().build()
    return _NC_CACHE["nc"]


def kernel(**inputs):
    nc = _get_nc()
    in_maps = _stage_inputs(inputs)
    res = run_bass_kernel_spmd(nc, in_maps, core_ids=list(range(8)))
    out = np.zeros((2, S, D), dtype=np.float32)
    for core in range(8):
        b, c = divmod(core, 4)
        o = res.results[core]["out"].reshape(2048, 1024)
        qrows = ((4 * np.arange(16) + c)[:, None] * 128 +
                 np.arange(128)[None, :]).reshape(-1)
        out[b, qrows] = o
    return out


# version 07
# speedup vs baseline: 1.0869x; 1.0869x over previous
import ml_dtypes
import numpy as np
import concourse.bass as bass
import concourse.mybir as mybir
from concourse.bass_utils import run_bass_kernel_spmd

F32 = mybir.dt.float32
BF16 = mybir.dt.bfloat16
Act = mybir.ActivationFunctionType
Alu = mybir.AluOpType
AX = mybir.AxisListType

D = 1024
S = 8192
NKC = 16
NG = 8
NSLOT = 16
NEG_BIG = -3.0e38

X1_O = 0
STATS_O = 32768
IDB_O = 33024
S1S_O = 35328
S3B_O = 35840
QFM_O = 39936
WKBF_O = 72704
WVBF_O = 89088
XS_O = (105472, 121856)
XSB_O = (138240, 146432)
KST_O = (154624, 162816)
VST_O = (171008, 179200)
IDF_O = 187392
QSTG_O = 188416
KJ_O = (72704, 89088)
VJ_O = (105472, 121856)
PSB_O = (138240, 140288, 142336, 144384)
PTSB_O = (146432, 148480)
SMSK_O = (150528, 154624)
OACC_O = 158720
XQST_O = (191488, 195584)
TFIN_O = (199680, 203776)
DMASK_O = 207872
X1FM_O = 39936
OSB_O = (39936, 134144)
AFM_O = 56320
BFM_O = 121856
WC_O = 187392
WCB_O = (195584, 199680)
S1T_O = (121856, 123904)
HBUF_O = (187392, 189440)
HBUFB_O = (191488, 192512)
W3B_O = (187392, 191488)
W3BB_O = (195584, 197632)
CT2_O = (125952, 130048)
ARENA_BYTES = 211968

NEGM_C = 0
L_C = 16
NMT_C = 32
AD_C = 36
ALPHA_C = 40
RS_C = 44
LINV_C = 48
MUS_C = 50
MU_C = 52
VS_C = 54
RV_C = 56
LNV_C = 58
RSTD_C = 60
LNL_C = 62

DMA_SEMS = ("SM", "XS0", "XS1", "Q", "GK0", "GK1", "GV0", "GV1",
            "KJ0", "KJ1", "VJ0", "VJ1", "XQ0", "XQ1",
            "WC", "HB0", "HB1", "W30", "W31", "OUT")
ALL_SEMS = ("T", "V", "SC") + DMA_SEMS


class KB:

    def __init__(self, debug=False):
        self.debug = debug
        nc = bass.Bass("TRN2", target_bir_lowering=False)
        self.nc = nc
        dram = nc.dram_tensor
        self.in_aps = {}
        def din(name, shape, dtype=F32):
            self.in_aps[name] = dram(name, shape, dtype, kind="ExternalInput").ap()
            return self.in_aps[name]
        self.xkv = din("xkv", [NKC, 128, 8, 512])
        self.xqfm = din("xqfm", [128, 8, 2048])
        self.xqtok = din("xqtok", [NSLOT, 128, 1024])
        self.wkT = din("wkT", [128, 8, 1024])
        self.wvT = din("wvT", [128, 8, 1024])
        self.wblk = din("wblk", [32, 128, 2, 8, 128])
        self.h1t = din("h1t", [8, 32, 128, 512])
        self.h2t = din("h2t", [8, 32, 128, 512])
        self.w3t = din("w3t", [32, 128, 1024])
        self.s1l = din("s1l", [128, 32])
        self.s2l = din("s2l", [128, 32])
        self.sh1l = din("sh1l", [128, 32])
        self.sh2l = din("sh2l", [128, 32])
        self.s3bl = din("s3bl", [128, 1024])
        self.identf = din("identf", [128, 128])
        self.dmaskin = din("dmaskin", [2, 128, 1024], BF16)
        self.outd = dram("out", [NSLOT, 128, 1024], F32, kind="ExternalOutput").ap()
        self.kscr = dram("kscr", [NG, 128, 8, 1024], BF16, kind="Internal").ap()
        self.vscr = dram("vscr", [NG, 128, 8, 1024], BF16, kind="Internal").ap()
        if debug:
            self.dbg_x1 = dram("dbg_x1", [128, 16, 1024], BF16,
                               kind="ExternalOutput").ap()
            self.dbg_st = dram("dbg_st", [128, 64], F32,
                               kind="ExternalOutput").ap()
            self.dbg_k0 = dram("dbg_k0", [128, 8, 1024], BF16,
                               kind="ExternalOutput").ap()
            self.dbg_v0 = dram("dbg_v0", [128, 8, 1024], BF16,
                               kind="ExternalOutput").ap()
            self.dbg_qfm = dram("dbg_qfm", [128, 8, 2048], BF16,
                                kind="ExternalOutput").ap()

        self.ps = nc.alloc_psum_tensor("ps", [128, 4096], F32).ap()
        nc.psum_base = 7
        self.psbf = nc.alloc_psum_tensor("psbf", [128, 1024], BF16).ap()

        arena = nc.alloc_sbuf_tensor("arena", [128, ARENA_BYTES], mybir.dt.uint8)
        base = nc.lookup_mloc(arena).addr
        def at(name, off, shape, dtype):
            return nc.alloc_sbuf_tensor_at(name, shape, dtype, offset=base + off).ap()
        self.x1 = at("x1", X1_O, [128, 16, 1024], BF16)
        self.st = at("stats", STATS_O, [128, 64], F32)
        self.idb = at("idb", IDB_O, [128, 128], BF16)
        self.dmask = at("dmask", DMASK_O, [128, 2, 1024], BF16)
        self.s1s = at("s1s", S1S_O, [128, 32], F32)
        self.s2s = at("s2s", S1S_O + 128, [128, 32], F32)
        self.sh1s = at("sh1s", S1S_O + 256, [128, 32], F32)
        self.sh2s = at("sh2s", S1S_O + 384, [128, 32], F32)
        self.s3b = at("s3b", S3B_O, [128, 1024], F32)
        self.qfm = at("qfm", QFM_O, [128, 8, 2048], BF16)
        self.wkbf = at("wkbf", WKBF_O, [128, 8, 1024], BF16)
        self.wvbf = at("wvbf", WVBF_O, [128, 8, 1024], BF16)
        self.xs = [at(f"xs{i}", XS_O[i], [128, 8, 512], F32) for i in range(2)]
        self.xsb = [at(f"xsb{i}", XSB_O[i], [128, 8, 512], BF16) for i in range(2)]
        self.kst = [at(f"kst{i}", KST_O[i], [128, 8, 512], BF16) for i in range(2)]
        self.vst = [at(f"vst{i}", VST_O[i], [128, 4, 1024], BF16) for i in range(2)]
        self.idf = at("idf", IDF_O, [128, 128], F32)
        self.qstg = at("qstg", QSTG_O, [128, 2, 2048], F32)
        self.kj = [at(f"kj{i}", KJ_O[i], [128, 8, 1024], BF16) for i in range(2)]
        self.vj = [at(f"vj{i}", VJ_O[i], [128, 8, 1024], BF16) for i in range(2)]
        self.psb = [at(f"psb{i}", PSB_O[i], [128, 1024], BF16) for i in range(4)]
        self.ptsb = [at(f"ptsb{i}", PTSB_O[i], [128, 1024], BF16) for i in range(2)]
        self.xqst = [at(f"xqst{i}", XQST_O[i], [128, 1024], F32) for i in range(2)]
        self.tfin = [at(f"tfin{i}", TFIN_O[i], [128, 1024], F32) for i in range(2)]
        self.smsk = [at(f"smsk{i}", SMSK_O[i], [128, 1024], F32) for i in range(2)]
        self.ct = self.smsk
        self.oacc = at("oacc", OACC_O, [128, 8, 1024], F32)
        self.x1fm = at("x1fm", X1FM_O, [128, 8, 1024], BF16)
        self.osb = [at(f"osb{i}", OSB_O[i], [128, 4, 1024], F32)
                    for i in range(2)]
        self.afm = at("afm", AFM_O, [128, 32, 1024], BF16)
        self.bfm = at("bfm", BFM_O, [128, 32, 1024], BF16)
        self.wc = at("wc", WC_O, [128, 2, 8, 128], F32)
        self.wcb = [at(f"wcb{i}", WCB_O[i], [128, 2, 8, 128], BF16) for i in range(2)]
        self.s1t = [at(f"s1t{i}", S1T_O[i], [128, 512], F32) for i in range(2)]
        self.hbuf = [at(f"hbuf{i}", HBUF_O[i], [128, 512], F32) for i in range(2)]
        self.hbufb = [at(f"hbufb{i}", HBUFB_O[i], [128, 512], BF16) for i in range(2)]
        self.w3b = [at(f"w3b{i}", W3B_O[i], [128, 1024], F32) for i in range(2)]
        self.w3bb = [at(f"w3bb{i}", W3BB_O[i], [128, 1024], BF16) for i in range(2)]
        self.ct2 = [at(f"ct2{i}", CT2_O[i], [128, 1024], F32) for i in range(2)]

        self.sems = {n: nc.alloc_semaphore(name=n, num=160 + i)
                     for i, n in enumerate(ALL_SEMS)}
        self.cnt = {n: 0 for n in self.sems}
        self.steps = []

    def emit(self, eng, fn):
        self.steps.append((eng, fn))

    def wait(self, eng, sem, val):
        if val <= 0:
            return
        h = self.sems[sem]
        self.emit(eng, lambda e, h=h, v=val: e.wait_ge(h, v))

    def wait_lazy(self, eng, sem, arr, idx):
        h = self.sems[sem]
        def fn(e, h=h, arr=arr, idx=idx):
            v = arr[idx]
            assert v > 0, (sem, idx)
            e.wait_ge(h, v)
        self.emit(eng, fn)

    def op(self, eng, fn, sem=None, n=1):
        if eng == "vector":
            self.wait(eng, "V", self.cnt["V"])
        elif eng == "scalar":
            self.wait(eng, "SC", self.cnt["SC"])
        if sem is None:
            self.emit(eng, lambda e, fn=fn: fn(e))
            return None
        self.cnt[sem] += n
        h = self.sems[sem]
        self.emit(eng, lambda e, fn=fn, h=h, n=n: fn(e).then_inc(h, n))
        return self.cnt[sem]

    def dma(self, eng, out, in_, sem):
        return self.op(eng, lambda e, o=out, i=in_: e.dma_start(out=o, in_=i),
                       sem, 16)

    def barrier_waits(self, eng, marks):
        for s, v in marks.items():
            self.wait(eng, s, v)

    def snap(self):
        return dict(self.cnt)

    def phase1(self):
        st, ps = self.st, self.ps
        for out, in_ in ((self.idf[:], self.identf),
                         (self.s1s[:], self.s1l), (self.s2s[:], self.s2l),
                         (self.sh1s[:], self.sh1l), (self.sh2s[:], self.sh2l),
                         (self.s3b[:], self.s3bl),
                         (self.dmask[:], self.dmaskin.rearrange("v p f -> p v f"))):
            self.wait("sync", "SM", self.cnt["SM"])
            c_small = self.dma("sync", out, in_, "SM")
        self.c_small = c_small
        v_wk = [0, 0, 0, 0]
        for w, (src, dstbf) in enumerate(((self.wkT, self.wkbf),
                                          (self.wvT, self.wvbf))):
            for h in range(2):
                stg = self.xs[h]
                stgv = stg.rearrange("p a f -> p (a f)").rearrange(
                    "p (a f) -> p a f", a=4)
                if w == 1:
                    self.wait("sync", "V", v_wk[h])
                cl = self.dma("sync", stgv, src[:, 4 * h:4 * h + 4, :],
                              f"XS{h}")
                self.wait("vector", f"XS{h}", cl)
                cv = self.op("vector", lambda e, s=stgv, d=dstbf, h=h:
                             e.tensor_copy(out=d[:, 4 * h:4 * h + 4, :], in_=s),
                             "V")
                v_wk[w * 2 + h] = cv
        self.wait("vector", "SM", c_small)
        self.v_id = self.op("vector", lambda e: e.tensor_copy(
            out=self.idb[:], in_=self.idf[:]), "V")

        t_xsb_used = [0] * NKC
        v_cast = [0] * NKC
        gp_k = [0] * NKC
        gp_v = [0] * NKC
        in_x = [0] * NKC
        v_kevac = {}
        s_vevac = {}
        qf_done = 0

        for kc in range(NKC):
            slot = kc % 2
            if kc >= 2:
                self.wait("sync", "V", v_cast[kc - 2])
            if kc < 2:
                self.wait("sync", "V", v_wk[2 + kc])
            in_x[kc] = self.dma("sync", self.xs[slot][:], self.xkv[kc],
                                f"XS{slot}")
            if kc in (1, 3, 5, 7):
                q = (kc - 1) // 2
                cq = self.dma("sync", self.qstg[:],
                              self.xqfm[:, 2 * q:2 * q + 2, :], "Q")
                self.wait("vector", "Q", cq)
                qf_done = self.op(
                    "vector", lambda e, q=q:
                    e.tensor_copy(out=self.qfm[:, 2 * q:2 * q + 2, :],
                                  in_=self.qstg[:]), "V")
                self.wait("sync", "V", qf_done)

            self.wait("vector", f"XS{slot}", in_x[kc])
            if kc >= 2:
                self.wait("vector", "T", t_xsb_used[kc - 2])
            v_cast[kc] = self.op(
                "vector", lambda e, slot=slot:
                e.tensor_copy(out=self.xsb[slot][:], in_=self.xs[slot][:]),
                "V")

            self.wait("tensor", "V", v_cast[kc])
            for od in range(8):
                bank = od % 2
                if (kc, od - 2) in v_kevac:
                    self.wait("tensor", "V", v_kevac[(kc, od - 2)])
                elif kc > 0 and od < 2:
                    self.wait("tensor", "V", v_kevac[(kc - 1, od + 6)])
                for dk in range(8):
                    self.op("tensor", lambda e, od=od, dk=dk, slot=slot, bank=bank:
                            e.matmul(self.ps[:, bank * 512:(bank + 1) * 512],
                                     self.wkbf[:, dk, od * 128:(od + 1) * 128],
                                     self.xsb[slot][:, dk, :],
                                     start=(dk == 0), stop=(dk == 7)),
                            "T" if dk == 7 else None)
                tmark = self.cnt["T"]
                self.wait("vector", "T", tmark)
                if kc >= 2 and od == 0:
                    self.wait("vector", f"GK{slot}", gp_k[kc - 2])
                v_kevac[(kc, od)] = self.op(
                    "vector", lambda e, slot=slot, od=od, bank=bank:
                    e.tensor_copy(out=self.kst[slot][:, od, :],
                                  in_=self.ps[:, bank * 512:(bank + 1) * 512]),
                    "V")
            for i in range(8):
                ts, oh = divmod(i, 2)
                bank = 2 + i % 2
                if (kc, i - 2) in s_vevac:
                    self.wait("tensor", "SC", s_vevac[(kc, i - 2)])
                elif kc > 0 and i < 2:
                    self.wait("tensor", "SC", s_vevac[(kc - 1, i + 6)])
                for dk in range(8):
                    self.op("tensor", lambda e, ts=ts, oh=oh, dk=dk, slot=slot, bank=bank:
                            e.matmul(self.ps[:, bank * 512:(bank + 1) * 512],
                                     self.xsb[slot][:, dk, ts * 128:(ts + 1) * 128],
                                     self.wvbf[:, dk, oh * 512:(oh + 1) * 512],
                                     start=(dk == 0), stop=(dk == 7)),
                            "T" if dk == 7 else None)
                tmark = self.cnt["T"]
                self.wait("scalar", "T", tmark)
                if kc >= 2 and i == 0:
                    self.wait("scalar", f"GV{slot}", gp_v[kc - 2])
                s_vevac[(kc, i)] = self.op(
                    "scalar", lambda e, slot=slot, ts=ts, oh=oh, bank=bank:
                    e.copy(out=self.vst[slot][:, ts, oh * 512:(oh + 1) * 512],
                           in_=self.ps[:, bank * 512:(bank + 1) * 512]),
                    "SC")
            t_xsb_used[kc] = self.cnt["T"]
            jg, jh = divmod(kc, 2)
            self.wait("gpsimd", "V", v_kevac[(kc, 7)])
            gp_k[kc] = self.dma("gpsimd",
                                self.kscr[jg][:, :, jh * 512:(jh + 1) * 512],
                                self.kst[slot][:], f"GK{slot}")
            self.wait("gpsimd", "SC", s_vevac[(kc, 7)])
            gp_v[kc] = self.dma("gpsimd",
                                self.vscr[jg][:, jh * 4:(jh + 1) * 4, :],
                                self.vst[slot][:], f"GV{slot}")

        self.qf_done = qf_done

    def phase2(self, p1_marks):
        st, ps = self.st, self.ps
        nc = self.nc
        from concourse.hw_specs import get_activation_tables
        tabs = list(get_activation_tables(nc.m.arch))
        nle = tabs.index("natural_log_exp_and_others")
        def load_tab(e, idx=nle):
            return e.add_instruction(mybir.InstLoadActFuncSet(
                name=self.nc.get_next_instruction_name(),
                act_func_set_id=idx, ins=[], outs=[]))
        self.op("scalar", load_tab)

        for eng in ("sync", "gpsimd", "tensor", "vector", "scalar"):
            self.barrier_waits(eng, p1_marks)
        dm_ld = self.c_small
        self.dm_ld = dm_ld

        items = []
        for sw in range(2):
            for j in range(4 if sw == 0 else 8):
                ms = [m for m in range(8 * sw, 8 * sw + 8) if m >= 2 * j]
                items.append((sw, j, ms))
        n_items = len(items)
        ld_k = [0] * n_items
        ld_v = [0] * n_items
        t_av_done = [0] * n_items
        issued = [0]

        def issue_load(i):
            j = items[i][1]
            buf = i % 2
            if i >= 2:
                self.wait_lazy("sync", "T", t_av_done, i - 2)
                self.wait_lazy("gpsimd", "T", t_av_done, i - 2)
            ld_k[i] = self.dma("sync", self.kj[buf][:], self.kscr[j],
                               f"KJ{buf}")
            ld_v[i] = self.dma("gpsimd", self.vj[buf][:], self.vscr[j],
                               f"VJ{buf}")

        pair = 0
        s_exp = {}
        s_exp_m = {}
        t_tr = {}
        s_pt = {}
        t_av = {}
        v_lup = {}
        v_oacc = {}
        oacc_free = [0] * 8
        bank_rel = {}
        avh_rel = {}
        smsk_busy = [0, 0]
        x1ready = [0] * NSLOT
        gp_xq = [0] * NSLOT
        v_xq_used = [0, 0]
        v_xq_used_slot = [0] * NSLOT

        def finalize(m):
            par = m % 2
            msl = m % 8
            self.wait("scalar", "V", v_oacc[m])
            self.op("scalar", lambda e, m=m, par=par:
                    e.activation(out=st[:, LNL_C + par:LNL_C + par + 1],
                                 in_=st[:, L_C + m:L_C + m + 1],
                                 func=Act.Ln), "SC")
            self.op("scalar", lambda e, par=par:
                    e.activation(out=st[:, LINV_C + par:LINV_C + par + 1],
                                 in_=st[:, LNL_C + par:LNL_C + par + 1],
                                 func=Act.Exp, scale=-1.0), "SC")
            c = self.op("scalar", lambda e, par=par, msl=msl:
                        e.activation(out=self.tfin[par][:],
                                     in_=self.oacc[:, msl, :],
                                     func=Act.Copy,
                                     scale=st[:, LINV_C + par:LINV_C + par + 1]),
                        "SC")
            oacc_free[msl] = c
            self.wait("vector", "SC", c)
            self.wait("vector", f"XQ{par}", gp_xq[m])
            v_xq_used[par] = self.op(
                "vector", lambda e, par=par:
                e.scalar_tensor_tensor(out=self.tfin[par][:],
                                       in0=self.xqst[par][:], scalar=2.0,
                                       in1=self.tfin[par][:],
                                       op0=Alu.mult, op1=Alu.add,
                                       accum_out=st[:, MUS_C + par:MUS_C + par + 1]),
                "V")
            v_xq_used_slot[m] = v_xq_used[par]
            self.wait("scalar", "V", v_xq_used[par])
            self.op("scalar", lambda e, par=par:
                    e.activation(out=st[:, MU_C + par:MU_C + par + 1],
                                 in_=st[:, MUS_C + par:MUS_C + par + 1],
                                 func=Act.Copy, scale=-1.0 / 1024.0), "SC")
            self.op("scalar", lambda e, par=par:
                    e.activation(out=self.ct[par][:], in_=self.tfin[par][:],
                                 func=Act.Identity,
                                 bias=st[:, MU_C + par:MU_C + par + 1]), "SC")
            self.op("scalar", lambda e, par=par:
                    e.activation(out=self.tfin[par][:], in_=self.ct[par][:],
                                 func=Act.Square,
                                 accum_out=st[:, VS_C + par:VS_C + par + 1]),
                    "SC")
            self.op("scalar", lambda e, par=par:
                    e.activation(out=st[:, RV_C + par:RV_C + par + 1],
                                 in_=st[:, VS_C + par:VS_C + par + 1],
                                 func=Act.Copy, scale=1.0 / 1024.0, bias=1e-5),
                    "SC")
            self.op("scalar", lambda e, par=par:
                    e.activation(out=st[:, LNV_C + par:LNV_C + par + 1],
                                 in_=st[:, RV_C + par:RV_C + par + 1],
                                 func=Act.Ln), "SC")
            self.op("scalar", lambda e, par=par:
                    e.activation(out=st[:, RSTD_C + par:RSTD_C + par + 1],
                                 in_=st[:, LNV_C + par:LNV_C + par + 1],
                                 func=Act.Exp, scale=-0.5), "SC")
            sm = self.op("scalar", lambda e, m=m, par=par:
                         e.activation(out=self.x1[:, m, :], in_=self.ct[par][:],
                                      func=Act.Copy,
                                      scale=st[:, RSTD_C + par:RSTD_C + par + 1]),
                         "SC")
            x1ready[m] = sm
            smsk_busy[par] = sm

        PEND = 3
        pend = []

        def emit_tail(ctx):
            p = ctx["pair"]
            m, mi, buf = ctx["m"], ctx["mi"], ctx["buf"]
            p2, p4 = ctx["p2"], ctx["p4"]
            i, first, diag, msl = ctx["i"], ctx["first"], ctx["diag"], ctx["msl"]
            self.wait("tensor", "SC", s_exp[p])
            if p >= 1:
                self.wait("tensor", "SC", s_pt[p - 1])
            for s4 in range(8):
                self.op("tensor", lambda e, p4=p4, s4=s4:
                        e.transpose(
                            self.psbf[:, s4 * 128:(s4 + 1) * 128],
                            self.psb[p4][:, s4 * 128:(s4 + 1) * 128],
                            self.idb[:]),
                        "T" if s4 == 7 else None)
            t_tr[p] = self.cnt["T"]
            self.wait("scalar", "T", t_tr[p])
            if p >= 2:
                self.wait("scalar", "T", t_av[p - 2])
            s_pt[p] = self.op(
                "scalar", lambda e, p2=p2:
                e.copy(out=self.ptsb[p2][:], in_=self.psbf[:]), "SC")
            self.wait("tensor", "SC", s_pt[p])
            assert ld_v[i] > 0, (i, p)
            self.wait("tensor", f"VJ{buf}", ld_v[i])
            t_avh = [0, 0]
            for h in range(2):
                ah = 2 * p + h
                self.wait("tensor", "V", avh_rel.get(ah - 3, 0))
                bank = 4 + ah % 3
                for ks in range(8):
                    self.op("tensor",
                            lambda e, ks=ks, bank=bank, h=h, p2=p2, buf=buf:
                            e.matmul(
                                ps[:, bank * 512:(bank + 1) * 512],
                                self.ptsb[p2][:, ks * 128:(ks + 1) * 128],
                                self.vj[buf][:, ks, h * 512:(h + 1) * 512],
                                start=(ks == 0), stop=(ks == 7)),
                            "T" if ks == 7 else None)
                t_avh[h] = self.cnt["T"]
            t_av[p] = self.cnt["T"]
            t_av_done[i] = self.cnt["T"]
            if getattr(self, "_tap_out", 0):
                self.wait("vector", "OUT", self._tap_out)
            for h in range(2):
                ah = 2 * p + h
                bank = 4 + ah % 3
                self.wait("vector", "T", t_avh[h])
                if h == 0:
                    self.wait("vector", "SC",
                              oacc_free[msl] if first else s_exp[p])
                if first:
                    avh_rel[ah] = self.op(
                        "vector", lambda e, msl=msl, bank=bank, h=h:
                        e.tensor_copy(
                            out=self.oacc[:, msl, h * 512:(h + 1) * 512],
                            in_=ps[:, bank * 512:(bank + 1) * 512]),
                        "V")
                else:
                    avh_rel[ah] = self.op(
                        "vector", lambda e, msl=msl, bank=bank, h=h, p4=p4:
                        e.scalar_tensor_tensor(
                            out=self.oacc[:, msl, h * 512:(h + 1) * 512],
                            in0=self.oacc[:, msl, h * 512:(h + 1) * 512],
                            scalar=st[:, ALPHA_C + p4:ALPHA_C + p4 + 1],
                            in1=ps[:, bank * 512:(bank + 1) * 512],
                            op0=Alu.mult, op1=Alu.add), "V")
            v_oacc[m] = self.cnt["V"]
            if self.debug and p == getattr(self, "dbg_pair", -1):
                ds = getattr(self, "dbg_slot", 0)
                self.wait("gpsimd", "V", self.cnt["V"])
                self.wait("gpsimd", "SC", self.cnt["SC"])
                self.wait("gpsimd", "OUT", self.cnt["OUT"])
                self.dma("gpsimd", self.dbg_x1[:, 0:2, :].rearrange(
                    "p a f -> p (a f)").rearrange("p (a f) -> p a f", a=2),
                    self.oacc[:, 0:2, :], "OUT")
                self._tap_out = self.cnt["OUT"]
                self.wait("gpsimd", "OUT", self.cnt["OUT"])
                self.op("gpsimd", lambda e, ds=ds: e.dma_start(
                    out=self.dbg_st[:, 0:1], in_=st[:, ds:ds + 1],
                    allow_slow_non_contiguous=True), "OUT", 16)
                self.wait("gpsimd", "OUT", self.cnt["OUT"])
                self.op("gpsimd", lambda e, ds=ds: e.dma_start(
                    out=self.dbg_st[:, 1:2], in_=st[:, L_C + ds:L_C + ds + 1],
                    allow_slow_non_contiguous=True), "OUT", 16)
                self._tap_out = self.cnt["OUT"]
            if diag:
                finalize(m)

        for k0 in range(2):
            issue_load(k0)
        issued[0] = 2
        for i, (sw, j, ms) in enumerate(items):
                while pend and pend[0]["i"] <= i - 2:
                    emit_tail(pend.pop(0))
                buf = i % 2
                if 2 * j in ms:
                    for md in (2 * j, 2 * j + 1):
                        if md >= 2:
                            self.wait_lazy("gpsimd", "V", v_xq_used_slot,
                                           md - 2)
                        gp_xq[md] = self.dma("gpsimd", self.xqst[md % 2][:],
                                             self.xqtok[md], f"XQ{md % 2}")
                if i + 2 < n_items and issued[0] <= i + 2:
                    issue_load(i + 2)
                    issued[0] = i + 3
                for idx, m in enumerate(ms):
                    mi = m % 2
                    p2 = pair % 2
                    p4 = pair % 4
                    diag = (j == m // 2)
                    first = (j == 0)
                    sbank = 2 * p2
                    assert ld_k[i] > 0, (i, pair)
                    self.wait("tensor", f"KJ{buf}", ld_k[i])
                    if pair >= 2:
                        rsem, rval = bank_rel[pair - 2]
                        self.wait("tensor", rsem, rval)
                    if pair == 0:
                        self.wait("tensor", "V", self.qf_done)
                    for dk in range(8):
                        for kh in range(2):
                            self.op("tensor",
                                    lambda e, dk=dk, kh=kh, m=m, buf=buf, sbank=sbank:
                                    e.matmul(
                                        ps[:, (sbank + kh) * 512:(sbank + kh + 1) * 512],
                                        self.qfm[:, dk, m * 128:(m + 1) * 128],
                                        self.kj[buf][:, dk, kh * 512:(kh + 1) * 512],
                                        start=(dk == 0), stop=(dk == 7)),
                                    "T" if (dk == 7 and kh == 1) else None)
                    t_sc = self.cnt["T"]
                    self.wait("vector", "T", t_sc)
                    if getattr(self, "_tap_out", 0):
                        self.wait("vector", "OUT", self._tap_out)
                    if not first:
                        if pair >= 4:
                            self.wait("vector", "SC", s_exp[pair - 4])
                        self.wait("vector", "SC", s_exp_m.get(m, 0))
                    sc_src = ps[:, sbank * 512:(sbank + 2) * 512]
                    if diag:
                        self.wait("vector", "SC", smsk_busy[mi])
                        self.wait("vector", "SM", dm_ld)
                        mk = self.op("vector", lambda e, sbank=sbank, mi=mi:
                                     e.tensor_add(out=self.smsk[mi][:],
                                                  in0=ps[:, sbank * 512:(sbank + 2) * 512],
                                                  in1=self.dmask[:, mi, :]), "V")
                        bank_rel[pair] = ("V", mk)
                        sc_src = self.smsk[mi][:]
                    if first:
                        vmark = self.op(
                            "vector", lambda e, m=m, sc_src=sc_src:
                            e.reduce_max(out=st[:, NEGM_C + m:NEGM_C + m + 1],
                                         in_=sc_src,
                                         axis=AX.X, negate=True), "V")
                    else:
                        self.op("vector", lambda e, p4=p4, sc_src=sc_src:
                                e.reduce_max(out=st[:, NMT_C + p4:NMT_C + p4 + 1],
                                             in_=sc_src,
                                             axis=AX.X, negate=True), "V")
                        self.op("vector", lambda e, p4=p4, m=m:
                                e.tensor_scalar(out=st[:, AD_C + p4:AD_C + p4 + 1],
                                                in0=st[:, NMT_C + p4:NMT_C + p4 + 1],
                                                scalar1=st[:, NEGM_C + m:NEGM_C + m + 1],
                                                scalar2=0.0,
                                                op0=Alu.subtract, op1=Alu.min), "V")
                        vmark = self.op(
                            "vector", lambda e, p4=p4, m=m:
                            e.tensor_scalar(out=st[:, NEGM_C + m:NEGM_C + m + 1],
                                            in0=st[:, AD_C + p4:AD_C + p4 + 1],
                                            scalar1=st[:, NEGM_C + m:NEGM_C + m + 1],
                                            scalar2=None,
                                            op0=Alu.add), "V")
                    self.wait("scalar", "V", vmark)
                    if pair >= 4:
                        self.wait("scalar", "T", t_tr[pair - 4])
                    if not first:
                        self.op("scalar", lambda e, p4=p4:
                                e.activation(out=st[:, ALPHA_C + p4:ALPHA_C + p4 + 1],
                                             in_=st[:, AD_C + p4:AD_C + p4 + 1],
                                             func=Act.Exp), "SC")
                    bias_c = NEGM_C + m
                    acc = st[:, L_C + m:L_C + m + 1] if first else \
                        st[:, RS_C + p4:RS_C + p4 + 1]
                    s_exp[pair] = self.op(
                        "scalar", lambda e, p4=p4, sc_src=sc_src, bias_c=bias_c, acc=acc:
                        e.activation(out=self.psb[p4][:],
                                     in_=sc_src,
                                     func=Act.Exp,
                                     bias=st[:, bias_c:bias_c + 1],
                                     accum_out=acc), "SC")
                    if diag:
                        smsk_busy[mi] = s_exp[pair]
                    else:
                        bank_rel[pair] = ("SC", s_exp[pair])
                    s_exp_m[m] = s_exp[pair]
                    if not first:
                        self.wait("vector", "SC", s_exp[pair])
                        v_lup[pair] = self.op(
                            "vector", lambda e, m=m, p4=p4:
                            e.scalar_tensor_tensor(
                                out=st[:, L_C + m:L_C + m + 1],
                                in0=st[:, L_C + m:L_C + m + 1],
                                scalar=st[:, ALPHA_C + p4:ALPHA_C + p4 + 1],
                                in1=st[:, RS_C + p4:RS_C + p4 + 1],
                                op0=Alu.mult, op1=Alu.add), "V")
                    if len(pend) == PEND:
                        emit_tail(pend.pop(0))
                    pend.append(dict(pair=pair, m=m, mi=mi, p2=p2, p4=p4,
                                     buf=buf, i=i, first=first, diag=diag,
                                     msl=m % 8, last=(idx == len(ms) - 1)))
                    pair += 1
        while pend:
            emit_tail(pend.pop(0))
        self.x1ready = x1ready
        if self.debug:
            self.wait("gpsimd", "T", self.cnt["T"])
            self.wait("gpsimd", "V", self.cnt["V"])
            self.wait("gpsimd", "OUT", self.cnt["OUT"])
            self.dma("gpsimd", self.dbg_qfm, self.qfm[:], "OUT")

    def phase3(self, p2_marks):
        st, ps = self.st, self.ps
        for eng in ("sync", "gpsimd", "tensor", "vector", "scalar"):
            self.barrier_waits(eng, p2_marks)

        from concourse.hw_specs import get_activation_tables
        tabs = list(get_activation_tables(self.nc.m.arch))
        def load_tab_f(idx):
            def f(e, idx=idx):
                return e.add_instruction(mybir.InstLoadActFuncSet(
                    name=self.nc.get_next_instruction_name(),
                    act_func_set_id=idx, ins=[], outs=[]))
            return f
        silu_i = tabs.index("silu_and_others")
        gelu_i = tabs.index("gelu_and_others")
        nle_i = tabs.index("natural_log_exp_and_others")

        gp_out = {}
        wguard = [0, 0]
        prev_s4_vadd = 0
        for t in range(2):
            sb = 8 * t
            v_tfe = [0] * 8
            for lm in range(8):
                m = sb + lm
                if lm >= 1:
                    self.wait("tensor", "V", v_tfe[lm - 1])
                if t == 1 and lm == 0:
                    self.wait("tensor", "OUT", gp_out[(0, 0, 3)])
                    self.wait("tensor", "V", prev_s4_vadd)
                for dd in range(8):
                    self.op("tensor", lambda e, m=m, dd=dd:
                            e.transpose(self.psbf[:, dd * 128:(dd + 1) * 128],
                                        self.x1[:, m, dd * 128:(dd + 1) * 128],
                                        self.idb[:]),
                            "T" if dd == 7 else None)
                tmark = self.cnt["T"]
                self.wait("vector", "T", tmark)
                v_tfe[lm] = self.op(
                    "vector", lambda e, lm=lm:
                    e.tensor_copy(
                        out=self.x1fm[:, :, lm * 128:(lm + 1) * 128],
                        in_=self.psbf.rearrange("p (a f) -> p a f", a=8)),
                    "V")
            self.op("scalar", load_tab_f(silu_i))
            v_wcast = [0] * 32
            t_mm1 = {}
            s_silu = {}
            v_astt = {}
            for c in range(32):
                if c >= 1:
                    self.wait("sync", "V", v_wcast[c - 1])
                else:
                    self.wait("sync", "V", wguard[0])
                    self.wait("sync", "T", wguard[1])
                cl = self.dma("sync", self.wc[:], self.wblk[c], "WC")
                self.wait("vector", "WC", cl)
                if c >= 2:
                    self.wait("vector", "T", t_mm1[(c - 2, 1)])
                elif wguard[1]:
                    self.wait("vector", "T", wguard[1])
                v_wcast[c] = self.op(
                    "vector", lambda e, c=c:
                    e.tensor_copy(out=self.wcb[c % 2][:], in_=self.wc[:]), "V")
                for th in range(2):
                    idx = c * 2 + th
                    b0 = (idx % 3) * 2
                    self.wait("tensor", "V", v_wcast[c])
                    if c == 0 and th == 0:
                        self.wait("tensor", "V", v_tfe[7])
                    if idx >= 3:
                        self.wait("tensor", "SC", s_silu[idx - 3])
                        self.wait("tensor", "V", v_astt[idx - 3])
                    for w in range(2):
                        for dd in range(8):
                            bank = b0 + w
                            self.op("tensor",
                                    lambda e, c=c, th=th, w=w, dd=dd, bank=bank:
                                    e.matmul(
                                        ps[:, bank * 512:(bank + 1) * 512],
                                        self.wcb[c % 2][:, w, dd, :],
                                        self.x1fm[:, dd, th * 512:(th + 1) * 512],
                                        start=(dd == 0), stop=(dd == 7)),
                                    "T" if (w == 1 and dd == 7) else None)
                    t_mm1[(c, th)] = self.cnt["T"]
                    self.wait("scalar", "T", t_mm1[(c, th)])
                    if idx >= 2:
                        self.wait("scalar", "V", v_astt[idx - 2])
                    s_silu[idx] = self.op(
                        "scalar", lambda e, c=c, idx=idx, b0=b0:
                        e.activation(out=self.s1t[idx % 2][:],
                                     in_=ps[:, b0 * 512:(b0 + 1) * 512],
                                     func=Act.Silu,
                                     scale=self.s1s[:, c:c + 1]), "SC")
                    self.wait("vector", "SC", s_silu[idx])
                    v_astt[idx] = self.op(
                        "vector", lambda e, c=c, th=th, idx=idx, b0=b0:
                        e.scalar_tensor_tensor(
                            out=self.afm[:, c, th * 512:(th + 1) * 512],
                            in0=ps[:, (b0 + 1) * 512:(b0 + 2) * 512],
                            scalar=self.s2s[:, c:c + 1],
                            in1=self.s1t[idx % 2][:],
                            op0=Alu.mult, op1=Alu.mult), "V")
            self.op("scalar", load_tab_f(gelu_i))
            if t == 1:
                self.wait("scalar", "OUT", gp_out[(0, 1, 3)])
            wguard = [v_wcast[31], t_mm1[(31, 1)]]
            for stage, (wt, src, dst, scl) in enumerate(
                    ((self.h1t, self.afm, self.bfm, self.sh1s),
                     (self.h2t, self.bfm, self.afm, self.sh2s))):
                v_hcast = {}
                t_hmm = {}
                s_gel = {}
                for sc in range(8):
                    for dd in range(32):
                        gi = sc * 32 + dd
                        slot = gi % 2
                        if gi >= 2:
                            self.wait("sync", "V", v_hcast[gi - 2])
                        else:
                            self.wait("sync", "V", wguard[0])
                            self.wait("sync", "T", wguard[1])
                        cl = self.dma("sync", self.hbuf[slot][:],
                                      wt[sc, dd], f"HB{slot}")
                        self.wait("vector", f"HB{slot}", cl)
                        if gi >= 2:
                            self.wait("vector", "T", t_hmm[gi - 2])
                        else:
                            self.wait("vector", "T", wguard[1])
                        v_hcast[gi] = self.op(
                            "vector", lambda e, slot=slot:
                            e.tensor_copy(out=self.hbufb[slot][:],
                                          in_=self.hbuf[slot][:]), "V")
                        self.wait("tensor", "V", v_hcast[gi])
                        if dd == 0 and sc == 0:
                            if stage == 0:
                                self.wait("tensor", "SC", s_silu[63])
                                self.wait("tensor", "V", v_astt[63])
                            else:
                                for c4 in range(4):
                                    self.wait("tensor", "SC", s_gel_prev[(7, c4)])
                        for c4 in range(4):
                            if dd == 0 and sc >= 1:
                                self.wait("tensor", "SC", s_gel[(sc - 1, c4)])
                            for th in range(2):
                                bank = c4 * 2 + th
                                self.op("tensor",
                                        lambda e, slot=slot, c4=c4, th=th, dd=dd,
                                        bank=bank, src=src:
                                        e.matmul(
                                            ps[:, bank * 512:(bank + 1) * 512],
                                            self.hbufb[slot][:, c4 * 128:(c4 + 1) * 128],
                                            src[:, dd, th * 512:(th + 1) * 512],
                                            start=(dd == 0), stop=(dd == 31)),
                                        "T" if (c4 == 3 and th == 1) else None)
                        t_hmm[gi] = self.cnt["T"]
                    for c4 in range(4):
                        self.wait("scalar", "T", t_hmm[sc * 32 + 31])
                        s_gel[(sc, c4)] = self.op(
                            "scalar", lambda e, sc=sc, c4=c4, dst=dst, scl=scl:
                            e.activation(out=dst[:, sc * 4 + c4, :],
                                         in_=ps[:, c4 * 1024:(c4 + 1) * 1024],
                                         func=Act.Gelu,
                                         scale=scl[:, sc * 4 + c4:sc * 4 + c4 + 1]),
                            "SC")
                s_gel_prev = s_gel
                wguard = [v_hcast[255], t_hmm[255]]
            self.op("scalar", load_tab_f(nle_i))
            v_add = {}
            v_w3cast = {}
            t_w3mm = {}
            for h in range(2):
                for ddb in range(4):
                    for dl in range(8):
                        dd = ddb * 8 + dl
                        gi = h * 32 + dd
                        slot = gi % 2
                        if gi >= 2:
                            self.wait("sync", "V", v_w3cast[gi - 2])
                        else:
                            self.wait("sync", "V", wguard[0])
                            self.wait("sync", "T", wguard[1])
                        cl = self.dma("sync", self.w3b[slot][:],
                                      self.w3t[dd], f"W3{slot}")
                        self.wait("vector", f"W3{slot}", cl)
                        if gi >= 2:
                            self.wait("vector", "T", t_w3mm[gi - 2])
                        else:
                            self.wait("vector", "T", wguard[1])
                        v_w3cast[gi] = self.op(
                            "vector", lambda e, slot=slot:
                            e.tensor_copy(out=self.w3bb[slot][:],
                                          in_=self.w3b[slot][:]), "V")
                        self.wait("tensor", "V", v_w3cast[gi])
                        if dl == 0:
                            if (h, ddb) == (0, 0):
                                for c4 in range(4):
                                    self.wait("tensor", "SC", s_gel_prev[(7, c4)])
                            else:
                                for tc in range(4):
                                    self.wait("tensor", "V",
                                              v_add[(h, ddb - 1, tc) if ddb > 0
                                                    else (h - 1, 3, tc)])
                        for tc in range(4):
                            for oh in range(2):
                                bank = tc * 2 + oh
                                self.op("tensor",
                                        lambda e, slot=slot, tc=tc, oh=oh, dd=dd,
                                        h=h, bank=bank, dl=dl:
                                        e.matmul(
                                            ps[:, bank * 512:(bank + 1) * 512],
                                            self.afm[:, dd, h * 512 + tc * 128:
                                                     h * 512 + (tc + 1) * 128],
                                            self.w3bb[slot][:, oh * 512:(oh + 1) * 512],
                                            start=(dl == 0), stop=(dl == 7)),
                                        "T" if (tc == 3 and oh == 1) else None)
                        t_w3mm[gi] = self.cnt["T"]
                    ob = self.osb[h]
                    for tc in range(4):
                        self.wait("vector", "T", t_w3mm[h * 32 + ddb * 8 + 7])
                        if ddb == 0:
                            if t == 1:
                                self.wait("vector", "OUT", gp_out[(0, h, 3)])
                            v_add[(h, ddb, tc)] = self.op(
                                "vector", lambda e, tc=tc, ob=ob:
                                e.tensor_copy(out=ob[:, tc, :],
                                              in_=ps[:, tc * 1024:(tc + 1) * 1024]),
                                "V")
                        else:
                            v_add[(h, ddb, tc)] = self.op(
                                "vector", lambda e, tc=tc, ob=ob:
                                e.tensor_add(out=ob[:, tc, :],
                                             in0=ob[:, tc, :],
                                             in1=ps[:, tc * 1024:(tc + 1) * 1024]),
                                "V")
                ob = self.osb[h]
                for tc in range(4):
                    m = sb + h * 4 + tc
                    par = tc % 2
                    self.op("vector", lambda e, tc=tc, ob=ob:
                            e.tensor_mul(out=ob[:, tc, :],
                                         in0=ob[:, tc, :],
                                         in1=self.s3b[:]), "V")
                    self.op("vector", lambda e, tc=tc, m=m, par=par, ob=ob:
                            e.scalar_tensor_tensor(out=self.ct2[par][:],
                                                   in0=self.x1[:, m, :],
                                                   scalar=1.0,
                                                   in1=ob[:, tc, :],
                                                   op0=Alu.mult, op1=Alu.add),
                            "V")
                    self.op("vector", lambda e, par=par:
                            e.reduce_sum(out=st[:, MUS_C + par:MUS_C + par + 1],
                                         in_=self.ct2[par][:], axis=AX.X), "V")
                    self.op("vector", lambda e, par=par:
                            e.tensor_scalar_mul(out=st[:, MU_C + par:MU_C + par + 1],
                                                in0=st[:, MUS_C + par:MUS_C + par + 1],
                                                scalar1=1.0 / 1024.0), "V")
                    self.op("vector", lambda e, par=par:
                            e.tensor_scalar_sub(out=self.ct2[par][:],
                                                in0=self.ct2[par][:],
                                                scalar1=st[:, MU_C + par:MU_C + par + 1]),
                            "V")
                    self.op("vector", lambda e, par=par, tc=tc, ob=ob:
                            e.scalar_tensor_tensor(out=ob[:, tc, :],
                                                   in0=self.ct2[par][:], scalar=1.0,
                                                   in1=self.ct2[par][:],
                                                   op0=Alu.mult, op1=Alu.mult,
                                                   accum_out=st[:, VS_C + par:VS_C + par + 1]),
                            "V")
                    vmark = self.op(
                        "vector", lambda e, par=par:
                        e.tensor_scalar(out=st[:, RV_C + par:RV_C + par + 1],
                                        in0=st[:, VS_C + par:VS_C + par + 1],
                                        scalar1=1.0 / 1024.0, scalar2=1e-5,
                                        op0=Alu.mult, op1=Alu.add), "V")
                    self.wait("scalar", "V", vmark)
                    self.op("scalar", lambda e, par=par:
                            e.activation(out=st[:, LNV_C + par:LNV_C + par + 1],
                                         in_=st[:, RV_C + par:RV_C + par + 1],
                                         func=Act.Ln), "SC")
                    smark = self.op(
                        "scalar", lambda e, par=par:
                        e.activation(out=st[:, RSTD_C + par:RSTD_C + par + 1],
                                     in_=st[:, LNV_C + par:LNV_C + par + 1],
                                     func=Act.Exp, scale=-0.5), "SC")
                    self.wait("vector", "SC", smark)
                    vmark = self.op(
                        "vector", lambda e, tc=tc, par=par, ob=ob:
                        e.tensor_scalar_mul(out=ob[:, tc, :],
                                            in0=self.ct2[par][:],
                                            scalar1=st[:, RSTD_C + par:RSTD_C + par + 1]),
                        "V")
                    self.wait("gpsimd", "V", vmark)
                    self.wait("gpsimd", "OUT", self.cnt["OUT"])
                    gp_out[(t, h, tc)] = self.dma(
                        "gpsimd", self.outd[m], ob[:, tc, :], "OUT")
            wguard = [v_w3cast[63], t_w3mm[63]]
            prev_s4_vadd = v_add[(1, 3, 3)]
        if self.debug:
            self.wait("gpsimd", "V", self.cnt["V"])
            self.wait("gpsimd", "SC", self.cnt["SC"])
            self.wait("gpsimd", "OUT", self.cnt["OUT"])
            if getattr(self, "dbg_pair", -1) < 0:
                self.dma("gpsimd", self.dbg_x1, self.x1[:], "OUT")
            self.wait("gpsimd", "OUT", self.cnt["OUT"])
            if getattr(self, "dbg_pair", -1) < 0:
                self.dma("gpsimd", self.dbg_st[:, 0:32], self.st[:, 0:32], "OUT")
            self.wait("gpsimd", "OUT", self.cnt["OUT"])
            m1 = self.dma("gpsimd", self.kj[0][:], self.kscr[0], "OUT")
            self.wait("gpsimd", "OUT", m1)
            self.dma("gpsimd", self.dbg_k0, self.kj[0][:], "OUT")
            self.wait("gpsimd", "OUT", self.cnt["OUT"])
            m2 = self.dma("gpsimd", self.vj[0][:], self.vscr[0], "OUT")
            self.wait("gpsimd", "OUT", m2)
            self.dma("gpsimd", self.dbg_v0, self.vj[0][:], "OUT")
        self.wait("gpsimd", "OUT", self.cnt["OUT"])

    def build(self):
        self.phase1()
        p1 = self.snap()
        self.phase2(p1)
        p2 = self.snap()
        self.phase3(p2)
        steps = self.steps
        with self.nc.Block() as block:
            @block.sync
            def _(e):
                for eng, fn in steps:
                    if eng == "sync":
                        fn(e)

            @block.gpsimd
            def _(e):
                for eng, fn in steps:
                    if eng == "gpsimd":
                        fn(e)

            @block.tensor
            def _(e):
                for eng, fn in steps:
                    if eng == "tensor":
                        fn(e)

            @block.vector
            def _(e):
                for eng, fn in steps:
                    if eng == "vector":
                        fn(e)

            @block.scalar
            def _(e):
                for eng, fn in steps:
                    if eng == "scalar":
                        fn(e)
        return self.nc


def _stage_inputs(inp):
    f32 = np.float32
    x = np.ascontiguousarray(inp["L0_attn_x"], dtype=f32)
    W = np.ascontiguousarray(inp["L0_attn_W"], dtype=f32)
    wkT = np.ascontiguousarray(W[:D].T)
    wvT = np.ascontiguousarray(W[D:].T)
    wk_l = np.ascontiguousarray(wkT.reshape(8, 128, 1024).transpose(1, 0, 2))
    wv_l = np.ascontiguousarray(wvT.reshape(8, 128, 1024).transpose(1, 0, 2))
    w1 = np.asarray(inp["L0_ffn_w1_w"], dtype=f32)
    w2 = np.asarray(inp["L0_ffn_w2_w"], dtype=f32)
    wblk = np.ascontiguousarray(
        np.stack([w1.reshape(32, 128, 8, 128), w2.reshape(32, 128, 8, 128)],
                 axis=1).transpose(0, 4, 1, 3, 2))
    h1 = np.asarray(inp["L0_ffn_h1_w"], dtype=f32)
    h2 = np.asarray(inp["L0_ffn_h2_w"], dtype=f32)
    h1t = np.ascontiguousarray(h1.reshape(8, 512, 32, 128).transpose(0, 2, 3, 1))
    h2t = np.ascontiguousarray(h2.reshape(8, 512, 32, 128).transpose(0, 2, 3, 1))
    w3 = np.asarray(inp["L0_ffn_w3_w"], dtype=f32)
    w3t = np.ascontiguousarray(w3.T.reshape(32, 128, 1024))
    s1l = np.ascontiguousarray(np.asarray(inp["L0_ffn_w1_s"], f32).reshape(32, 128).T)
    s2l = np.ascontiguousarray(np.asarray(inp["L0_ffn_w2_s"], f32).reshape(32, 128).T)
    sh1l = np.ascontiguousarray(np.asarray(inp["L0_ffn_h1_s"], f32).reshape(32, 128).T)
    sh2l = np.ascontiguousarray(np.asarray(inp["L0_ffn_h2_s"], f32).reshape(32, 128).T)
    s3bl = np.ascontiguousarray(
        np.broadcast_to(np.asarray(inp["L0_ffn_w3_s"], f32)[None, :], (128, 1024)))
    identf = np.eye(128, dtype=f32)

    in_maps = []
    for core in range(8):
        b, c = divmod(core, 4)
        xb = x[b]
        qrows = ((4 * np.arange(16) + c)[:, None] * 128 +
                 np.arange(128)[None, :]).reshape(-1)
        xq = np.ascontiguousarray(xb[qrows])
        xqfm = np.ascontiguousarray(
            xq.T.reshape(8, 128, 2048).transpose(1, 0, 2))
        xqtok = np.ascontiguousarray(xq.reshape(16, 128, 1024))
        xkv = np.ascontiguousarray(
            xb.reshape(16, 512, 8, 128).transpose(0, 3, 2, 1))
        i = np.arange(128)[:, None]
        kk = np.arange(1024)[None, :]
        dmsk = np.stack([
            np.where(kk <= v * 512 + c * 128 + i, np.float32(0.0),
                     np.float32(NEG_BIG))
            for v in range(2)]).astype(ml_dtypes.bfloat16)
        in_maps.append(dict(
            xkv=xkv, xqfm=xqfm, xqtok=xqtok, wkT=wk_l, wvT=wv_l,
            wblk=wblk, h1t=h1t, h2t=h2t, w3t=w3t,
            s1l=s1l, s2l=s2l, sh1l=sh1l, sh2l=sh2l, s3bl=s3bl,
            identf=identf, dmaskin=dmsk))
    return in_maps


_NC_CACHE = {}


def _get_nc():
    if "nc" not in _NC_CACHE:
        _NC_CACHE["nc"] = KB().build()
    return _NC_CACHE["nc"]


def kernel(**inputs):
    nc = _get_nc()
    in_maps = _stage_inputs(inputs)
    res = run_bass_kernel_spmd(nc, in_maps, core_ids=list(range(8)))
    out = np.zeros((2, S, D), dtype=np.float32)
    for core in range(8):
        b, c = divmod(core, 4)
        o = res.results[core]["out"].reshape(2048, 1024)
        qrows = ((4 * np.arange(16) + c)[:, None] * 128 +
                 np.arange(128)[None, :]).reshape(-1)
        out[b, qrows] = o
    return out


# version 09
# speedup vs baseline: 1.1268x; 1.0366x over previous
import ml_dtypes
import numpy as np
import concourse.bass as bass
import concourse.mybir as mybir
from concourse.bass_utils import run_bass_kernel_spmd

F32 = mybir.dt.float32
BF16 = mybir.dt.bfloat16
Act = mybir.ActivationFunctionType
Alu = mybir.AluOpType
AX = mybir.AxisListType

D = 1024
S = 8192
NKC = 16
NG = 8
NSLOT = 16
NEG_BIG = -3.0e38

X1_O = 0
STATS_O = 32768
IDB_O = 33024
S1S_O = 35328
S3B_O = 35840
QFM_O = 39936
WKBF_O = 72704
WVBF_O = 89088
XS_O = (105472, 121856)
XSB_O = (138240, 146432)
KST_O = (154624, 162816)
VST_O = (171008, 179200)
IDF_O = 187392
QSTG_O = 188416
KJ_O = (72704, 89088)
VJ_O = (105472, 121856)
PSB_O = (138240, 140288, 142336, 144384)
PTSB_O = (146432, 148480)
SMSK_O = (150528, 154624)
OACC_O = 158720
XQST_O = (191488, 195584)
TFIN_O = (199680, 203776)
DMASK_O = 207872
X1FM_O = 39936
OSB_O = (39936, 134144)
AFM_O = 56320
BFM_O = 121856
WC_O = 187392
WCB_O = (195584, 199680)
S1T_O = (121856, 123904)
HBUF_O = (187392, 189440)
HBUFB_O = (191488, 192512)
W3B_O = (187392, 191488)
W3BB_O = (195584, 197632)
CT2_O = (125952, 130048)
ARENA_BYTES = 211968

NEGM_C = 0
L_C = 16
NMT_C = 32
AD_C = 36
ALPHA_C = 40
RS_C = 44
LINV_C = 48
MUS_C = 50
MU_C = 52
VS_C = 54
RV_C = 56
LNV_C = 58
RSTD_C = 60
LNL_C = 62

DMA_SEMS = ("SM", "XS0", "XS1", "Q", "GK0", "GK1", "GV0", "GV1",
            "KJ0", "KJ1", "VJ0", "VJ1", "XQ0", "XQ1",
            "WC", "HB0", "HB1", "W30", "W31", "OUT")
ALL_SEMS = ("T", "V", "SC") + DMA_SEMS


class KB:

    def __init__(self, debug=False):
        self.debug = debug
        nc = bass.Bass("TRN2", target_bir_lowering=False)
        self.nc = nc
        dram = nc.dram_tensor
        self.in_aps = {}
        def din(name, shape, dtype=F32):
            self.in_aps[name] = dram(name, shape, dtype, kind="ExternalInput").ap()
            return self.in_aps[name]
        self.xkv = din("xkv", [NKC, 128, 8, 512])
        self.xqfm = din("xqfm", [128, 8, 2048])
        self.xqtok = din("xqtok", [NSLOT, 128, 1024])
        self.wkT = din("wkT", [128, 8, 1024])
        self.wvT = din("wvT", [128, 8, 1024])
        self.wblk = din("wblk", [32, 128, 2, 8, 128])
        self.h1t = din("h1t", [8, 32, 128, 512])
        self.h2t = din("h2t", [8, 32, 128, 512])
        self.w3t = din("w3t", [32, 128, 1024])
        self.s1l = din("s1l", [128, 32])
        self.s2l = din("s2l", [128, 32])
        self.sh1l = din("sh1l", [128, 32])
        self.sh2l = din("sh2l", [128, 32])
        self.s3bl = din("s3bl", [128, 1024])
        self.identf = din("identf", [128, 128])
        self.dmaskin = din("dmaskin", [2, 128, 1024], BF16)
        self.outd = dram("out", [NSLOT, 128, 1024], F32, kind="ExternalOutput").ap()
        self.kscr = dram("kscr", [NG, 128, 8, 1024], BF16, kind="Internal").ap()
        self.vscr = dram("vscr", [NG, 128, 8, 1024], BF16, kind="Internal").ap()
        if debug:
            self.dbg_x1 = dram("dbg_x1", [128, 16, 1024], BF16,
                               kind="ExternalOutput").ap()
            self.dbg_st = dram("dbg_st", [128, 64], F32,
                               kind="ExternalOutput").ap()
            self.dbg_k0 = dram("dbg_k0", [128, 8, 1024], BF16,
                               kind="ExternalOutput").ap()
            self.dbg_v0 = dram("dbg_v0", [128, 8, 1024], BF16,
                               kind="ExternalOutput").ap()
            self.dbg_qfm = dram("dbg_qfm", [128, 8, 2048], BF16,
                                kind="ExternalOutput").ap()

        self.ps = nc.alloc_psum_tensor("ps", [128, 4096], F32).ap()
        nc.psum_base = 7
        self.psbf = nc.alloc_psum_tensor("psbf", [128, 1024], BF16).ap()

        arena = nc.alloc_sbuf_tensor("arena", [128, ARENA_BYTES], mybir.dt.uint8)
        base = nc.lookup_mloc(arena).addr
        def at(name, off, shape, dtype):
            return nc.alloc_sbuf_tensor_at(name, shape, dtype, offset=base + off).ap()
        self.x1 = at("x1", X1_O, [128, 16, 1024], BF16)
        self.st = at("stats", STATS_O, [128, 64], F32)
        self.idb = at("idb", IDB_O, [128, 128], BF16)
        self.dmask = at("dmask", DMASK_O, [128, 2, 1024], BF16)
        self.s1s = at("s1s", S1S_O, [128, 32], F32)
        self.s2s = at("s2s", S1S_O + 128, [128, 32], F32)
        self.sh1s = at("sh1s", S1S_O + 256, [128, 32], F32)
        self.sh2s = at("sh2s", S1S_O + 384, [128, 32], F32)
        self.s3b = at("s3b", S3B_O, [128, 1024], F32)
        self.qfm = at("qfm", QFM_O, [128, 8, 2048], BF16)
        self.wkbf = at("wkbf", WKBF_O, [128, 8, 1024], BF16)
        self.wvbf = at("wvbf", WVBF_O, [128, 8, 1024], BF16)
        self.xs = [at(f"xs{i}", XS_O[i], [128, 8, 512], F32) for i in range(2)]
        self.xsb = [at(f"xsb{i}", XSB_O[i], [128, 8, 512], BF16) for i in range(2)]
        self.kst = [at(f"kst{i}", KST_O[i], [128, 8, 512], BF16) for i in range(2)]
        self.vst = [at(f"vst{i}", VST_O[i], [128, 4, 1024], BF16) for i in range(2)]
        self.idf = at("idf", IDF_O, [128, 128], F32)
        self.qstg = at("qstg", QSTG_O, [128, 2, 2048], F32)
        self.kj = [at(f"kj{i}", KJ_O[i], [128, 8, 1024], BF16) for i in range(2)]
        self.vj = [at(f"vj{i}", VJ_O[i], [128, 8, 1024], BF16) for i in range(2)]
        self.psb = [at(f"psb{i}", PSB_O[i], [128, 1024], BF16) for i in range(4)]
        self.ptsb = [at(f"ptsb{i}", PTSB_O[i], [128, 1024], BF16) for i in range(2)]
        self.xqst = [at(f"xqst{i}", XQST_O[i], [128, 1024], F32) for i in range(2)]
        self.tfin = [at(f"tfin{i}", TFIN_O[i], [128, 1024], F32) for i in range(2)]
        self.smsk = [at(f"smsk{i}", SMSK_O[i], [128, 1024], F32) for i in range(2)]
        self.oacc = at("oacc", OACC_O, [128, 8, 1024], F32)
        self.x1fm = at("x1fm", X1FM_O, [128, 8, 1024], BF16)
        self.osb = [at(f"osb{i}", OSB_O[i], [128, 4, 1024], F32)
                    for i in range(2)]
        self.afm = at("afm", AFM_O, [128, 32, 1024], BF16)
        self.bfm = at("bfm", BFM_O, [128, 32, 1024], BF16)
        self.wc = at("wc", WC_O, [128, 2, 8, 128], F32)
        self.wcb = [at(f"wcb{i}", WCB_O[i], [128, 2, 8, 128], BF16) for i in range(2)]
        self.s1t = [at(f"s1t{i}", S1T_O[i], [128, 512], F32) for i in range(2)]
        self.hbuf = [at(f"hbuf{i}", HBUF_O[i], [128, 512], F32) for i in range(2)]
        self.hbufb = [at(f"hbufb{i}", HBUFB_O[i], [128, 512], BF16) for i in range(2)]
        self.w3b = [at(f"w3b{i}", W3B_O[i], [128, 1024], F32) for i in range(2)]
        self.w3bb = [at(f"w3bb{i}", W3BB_O[i], [128, 1024], BF16) for i in range(2)]
        self.ct2 = [at(f"ct2{i}", CT2_O[i], [128, 1024], F32) for i in range(2)]

        self.sems = {n: nc.alloc_semaphore(name=n, num=160 + i)
                     for i, n in enumerate(ALL_SEMS)}
        self.cnt = {n: 0 for n in self.sems}
        self.steps = []

    def emit(self, eng, fn):
        self.steps.append((eng, fn))

    def wait(self, eng, sem, val):
        if val <= 0:
            return
        h = self.sems[sem]
        self.emit(eng, lambda e, h=h, v=val: e.wait_ge(h, v))

    def wait_lazy(self, eng, sem, arr, idx):
        h = self.sems[sem]
        def fn(e, h=h, arr=arr, idx=idx):
            v = arr[idx]
            assert v > 0, (sem, idx)
            e.wait_ge(h, v)
        self.emit(eng, fn)

    def op(self, eng, fn, sem=None, n=1):
        if eng == "vector":
            self.wait(eng, "V", self.cnt["V"])
        elif eng == "scalar":
            self.wait(eng, "SC", self.cnt["SC"])
        if sem is None:
            self.emit(eng, lambda e, fn=fn: fn(e))
            return None
        self.cnt[sem] += n
        h = self.sems[sem]
        self.emit(eng, lambda e, fn=fn, h=h, n=n: fn(e).then_inc(h, n))
        return self.cnt[sem]

    def dma(self, eng, out, in_, sem):
        return self.op(eng, lambda e, o=out, i=in_: e.dma_start(out=o, in_=i),
                       sem, 16)

    def barrier_waits(self, eng, marks):
        for s, v in marks.items():
            self.wait(eng, s, v)

    def snap(self):
        return dict(self.cnt)

    def phase1(self):
        st, ps = self.st, self.ps
        for out, in_ in ((self.idf[:], self.identf),
                         (self.s1s[:], self.s1l), (self.s2s[:], self.s2l),
                         (self.sh1s[:], self.sh1l), (self.sh2s[:], self.sh2l),
                         (self.s3b[:], self.s3bl),
                         (self.dmask[:], self.dmaskin.rearrange("v p f -> p v f"))):
            self.wait("sync", "SM", self.cnt["SM"])
            c_small = self.dma("sync", out, in_, "SM")
        self.c_small = c_small
        v_wk = [0, 0, 0, 0]
        for w, (src, dstbf) in enumerate(((self.wkT, self.wkbf),
                                          (self.wvT, self.wvbf))):
            for h in range(2):
                stg = self.xs[h]
                stgv = stg.rearrange("p a f -> p (a f)").rearrange(
                    "p (a f) -> p a f", a=4)
                if w == 1:
                    self.wait("sync", "V", v_wk[h])
                cl = self.dma("sync", stgv, src[:, 4 * h:4 * h + 4, :],
                              f"XS{h}")
                self.wait("vector", f"XS{h}", cl)
                cv = self.op("vector", lambda e, s=stgv, d=dstbf, h=h:
                             e.tensor_copy(out=d[:, 4 * h:4 * h + 4, :], in_=s),
                             "V")
                v_wk[w * 2 + h] = cv
        self.wait("vector", "SM", c_small)
        self.v_id = self.op("vector", lambda e: e.tensor_copy(
            out=self.idb[:], in_=self.idf[:]), "V")

        t_xsb_used = [0] * NKC
        v_cast = [0] * NKC
        gp_k = [0] * NKC
        gp_v = [0] * NKC
        in_x = [0] * NKC
        v_kevac = {}
        s_vevac = {}
        qf_done = 0

        for kc in range(NKC):
            slot = kc % 2
            if kc >= 2:
                self.wait("sync", "V", v_cast[kc - 2])
            if kc < 2:
                self.wait("sync", "V", v_wk[2 + kc])
            in_x[kc] = self.dma("sync", self.xs[slot][:], self.xkv[kc],
                                f"XS{slot}")
            if kc in (1, 3, 5, 7):
                q = (kc - 1) // 2
                cq = self.dma("sync", self.qstg[:],
                              self.xqfm[:, 2 * q:2 * q + 2, :], "Q")
                self.wait("vector", "Q", cq)
                qf_done = self.op(
                    "vector", lambda e, q=q:
                    e.tensor_copy(out=self.qfm[:, 2 * q:2 * q + 2, :],
                                  in_=self.qstg[:]), "V")
                self.wait("sync", "V", qf_done)

            self.wait("vector", f"XS{slot}", in_x[kc])
            if kc >= 2:
                self.wait("vector", "T", t_xsb_used[kc - 2])
            v_cast[kc] = self.op(
                "vector", lambda e, slot=slot:
                e.tensor_copy(out=self.xsb[slot][:], in_=self.xs[slot][:]),
                "V")

            self.wait("tensor", "V", v_cast[kc])
            for od in range(8):
                bank = od % 2
                if (kc, od - 2) in v_kevac:
                    self.wait("tensor", "V", v_kevac[(kc, od - 2)])
                elif kc > 0 and od < 2:
                    self.wait("tensor", "V", v_kevac[(kc - 1, od + 6)])
                for dk in range(8):
                    self.op("tensor", lambda e, od=od, dk=dk, slot=slot, bank=bank:
                            e.matmul(self.ps[:, bank * 512:(bank + 1) * 512],
                                     self.wkbf[:, dk, od * 128:(od + 1) * 128],
                                     self.xsb[slot][:, dk, :],
                                     start=(dk == 0), stop=(dk == 7)),
                            "T" if dk == 7 else None)
                tmark = self.cnt["T"]
                self.wait("vector", "T", tmark)
                if kc >= 2 and od == 0:
                    self.wait("vector", f"GK{slot}", gp_k[kc - 2])
                v_kevac[(kc, od)] = self.op(
                    "vector", lambda e, slot=slot, od=od, bank=bank:
                    e.tensor_copy(out=self.kst[slot][:, od, :],
                                  in_=self.ps[:, bank * 512:(bank + 1) * 512]),
                    "V")
            for i in range(8):
                ts, oh = divmod(i, 2)
                bank = 2 + i % 2
                if (kc, i - 2) in s_vevac:
                    self.wait("tensor", "SC", s_vevac[(kc, i - 2)])
                elif kc > 0 and i < 2:
                    self.wait("tensor", "SC", s_vevac[(kc - 1, i + 6)])
                for dk in range(8):
                    self.op("tensor", lambda e, ts=ts, oh=oh, dk=dk, slot=slot, bank=bank:
                            e.matmul(self.ps[:, bank * 512:(bank + 1) * 512],
                                     self.xsb[slot][:, dk, ts * 128:(ts + 1) * 128],
                                     self.wvbf[:, dk, oh * 512:(oh + 1) * 512],
                                     start=(dk == 0), stop=(dk == 7)),
                            "T" if dk == 7 else None)
                tmark = self.cnt["T"]
                self.wait("scalar", "T", tmark)
                if kc >= 2 and i == 0:
                    self.wait("scalar", f"GV{slot}", gp_v[kc - 2])
                s_vevac[(kc, i)] = self.op(
                    "scalar", lambda e, slot=slot, ts=ts, oh=oh, bank=bank:
                    e.copy(out=self.vst[slot][:, ts, oh * 512:(oh + 1) * 512],
                           in_=self.ps[:, bank * 512:(bank + 1) * 512]),
                    "SC")
            t_xsb_used[kc] = self.cnt["T"]
            jg, jh = divmod(kc, 2)
            self.wait("gpsimd", "V", v_kevac[(kc, 7)])
            gp_k[kc] = self.dma("gpsimd",
                                self.kscr[jg][:, :, jh * 512:(jh + 1) * 512],
                                self.kst[slot][:], f"GK{slot}")
            self.wait("gpsimd", "SC", s_vevac[(kc, 7)])
            gp_v[kc] = self.dma("gpsimd",
                                self.vscr[jg][:, jh * 4:(jh + 1) * 4, :],
                                self.vst[slot][:], f"GV{slot}")

        self.qf_done = qf_done

    def phase2(self, p1_marks):
        st, ps = self.st, self.ps
        nc = self.nc
        from concourse.hw_specs import get_activation_tables
        tabs = list(get_activation_tables(nc.m.arch))
        nle = tabs.index("natural_log_exp_and_others")
        def load_tab(e, idx=nle):
            return e.add_instruction(mybir.InstLoadActFuncSet(
                name=self.nc.get_next_instruction_name(),
                act_func_set_id=idx, ins=[], outs=[]))
        self.op("scalar", load_tab)

        for eng in ("sync", "gpsimd", "tensor", "vector", "scalar"):
            self.barrier_waits(eng, p1_marks)
        dm_ld = self.c_small
        self.dm_ld = dm_ld

        items = []
        for sw in range(2):
            for j in range(4 if sw == 0 else 8):
                ms = [m for m in range(8 * sw, 8 * sw + 8) if m >= 2 * j]
                items.append((sw, j, ms))
        n_items = len(items)
        ld_k = [0] * n_items
        ld_v = [0] * n_items
        t_av_done = [0] * n_items
        t_sc_done = [0] * n_items
        issued = [0]

        def issue_load(i):
            j = items[i][1]
            buf = i % 2
            if i >= 2:
                self.wait_lazy("sync", "T", t_sc_done, i - 2)
                self.wait_lazy("gpsimd", "T", t_av_done, i - 2)
            ld_k[i] = self.dma("sync", self.kj[buf][:], self.kscr[j],
                               f"KJ{buf}")
            ld_v[i] = self.dma("gpsimd", self.vj[buf][:], self.vscr[j],
                               f"VJ{buf}")

        pair = 0
        s_exp = {}
        s_exp_m = {}
        t_tr = {}
        s_pt = {}
        t_av = {}
        v_lup = {}
        v_oacc = {}
        oacc_free = [0] * 8
        bank_rel = {}
        avh_rel = {}
        smsk_busy = [0, 0]
        x1ready = [0] * NSLOT
        gp_xq = [0] * NSLOT
        v_xq_used = [0, 0]
        v_xq_used_slot = [0] * NSLOT
        fin_q = []
        last_fin = [-10]

        def finalize(m):
            par = m % 2
            msl = m % 8
            self.wait("scalar", "V", v_oacc[m])
            self.op("scalar", lambda e, m=m, par=par:
                    e.activation(out=st[:, LNL_C + par:LNL_C + par + 1],
                                 in_=st[:, L_C + m:L_C + m + 1],
                                 func=Act.Ln), "SC")
            self.op("scalar", lambda e, par=par:
                    e.activation(out=st[:, LINV_C + par:LINV_C + par + 1],
                                 in_=st[:, LNL_C + par:LNL_C + par + 1],
                                 func=Act.Exp, scale=-1.0), "SC")
            c = self.op("scalar", lambda e, par=par, msl=msl:
                        e.activation(out=self.tfin[par][:],
                                     in_=self.oacc[:, msl, :],
                                     func=Act.Copy,
                                     scale=st[:, LINV_C + par:LINV_C + par + 1]),
                        "SC")
            oacc_free[msl] = c
            self.wait("vector", "SC", c)
            self.wait("vector", f"XQ{par}", gp_xq[m])
            v_xq_used[par] = self.op(
                "vector", lambda e, par=par:
                e.scalar_tensor_tensor(out=self.tfin[par][:],
                                       in0=self.xqst[par][:], scalar=2.0,
                                       in1=self.tfin[par][:],
                                       op0=Alu.mult, op1=Alu.add,
                                       accum_out=st[:, MUS_C + par:MUS_C + par + 1]),
                "V")
            v_xq_used_slot[m] = v_xq_used[par]
            self.wait("scalar", "V", v_xq_used[par])
            self.op("scalar", lambda e, par=par:
                    e.activation(out=st[:, MU_C + par:MU_C + par + 1],
                                 in_=st[:, MUS_C + par:MUS_C + par + 1],
                                 func=Act.Copy, scale=-1.0 / 1024.0), "SC")
            self.op("scalar", lambda e, par=par:
                    e.activation(out=self.tfin[par][:], in_=self.tfin[par][:],
                                 func=Act.Identity,
                                 bias=st[:, MU_C + par:MU_C + par + 1]), "SC")
            self.op("scalar", lambda e, par=par:
                    e.activation(out=self.xqst[par][:], in_=self.tfin[par][:],
                                 func=Act.Square,
                                 accum_out=st[:, VS_C + par:VS_C + par + 1]),
                    "SC")
            self.op("scalar", lambda e, par=par:
                    e.activation(out=st[:, RV_C + par:RV_C + par + 1],
                                 in_=st[:, VS_C + par:VS_C + par + 1],
                                 func=Act.Copy, scale=1.0 / 1024.0, bias=1e-5),
                    "SC")
            self.op("scalar", lambda e, par=par:
                    e.activation(out=st[:, LNV_C + par:LNV_C + par + 1],
                                 in_=st[:, RV_C + par:RV_C + par + 1],
                                 func=Act.Ln), "SC")
            self.op("scalar", lambda e, par=par:
                    e.activation(out=st[:, RSTD_C + par:RSTD_C + par + 1],
                                 in_=st[:, LNV_C + par:LNV_C + par + 1],
                                 func=Act.Exp, scale=-0.5), "SC")
            sm = self.op("scalar", lambda e, m=m, par=par:
                         e.activation(out=self.x1[:, m, :],
                                      in_=self.tfin[par][:],
                                      func=Act.Copy,
                                      scale=st[:, RSTD_C + par:RSTD_C + par + 1]),
                         "SC")
            x1ready[m] = sm

        PEND = 3
        pend = []

        def emit_tail(ctx):
            p = ctx["pair"]
            m, mi, buf = ctx["m"], ctx["mi"], ctx["buf"]
            p2, p4 = ctx["p2"], ctx["p4"]
            i, first, diag, msl = ctx["i"], ctx["first"], ctx["diag"], ctx["msl"]
            self.wait("tensor", "SC", s_exp[p])
            if p >= 1:
                self.wait("tensor", "SC", s_pt[p - 1])
            for s4 in range(8):
                self.op("tensor", lambda e, p4=p4, s4=s4:
                        e.transpose(
                            self.psbf[:, s4 * 128:(s4 + 1) * 128],
                            self.psb[p4][:, s4 * 128:(s4 + 1) * 128],
                            self.idb[:]),
                        "T" if s4 == 7 else None)
            t_tr[p] = self.cnt["T"]
            self.wait("scalar", "T", t_tr[p])
            if p >= 2:
                self.wait("scalar", "T", t_av[p - 2])
            s_pt[p] = self.op(
                "scalar", lambda e, p2=p2:
                e.copy(out=self.ptsb[p2][:], in_=self.psbf[:]), "SC")
            self.wait("tensor", "SC", s_pt[p])
            assert ld_v[i] > 0, (i, p)
            self.wait("tensor", f"VJ{buf}", ld_v[i])
            t_avh = [0, 0]
            for h in range(2):
                ah = 2 * p + h
                self.wait("tensor", "V", avh_rel.get(ah - 3, 0))
                bank = 4 + ah % 3
                for ks in range(8):
                    self.op("tensor",
                            lambda e, ks=ks, bank=bank, h=h, p2=p2, buf=buf:
                            e.matmul(
                                ps[:, bank * 512:(bank + 1) * 512],
                                self.ptsb[p2][:, ks * 128:(ks + 1) * 128],
                                self.vj[buf][:, ks, h * 512:(h + 1) * 512],
                                start=(ks == 0), stop=(ks == 7)),
                            "T" if ks == 7 else None)
                t_avh[h] = self.cnt["T"]
            t_av[p] = self.cnt["T"]
            t_av_done[i] = self.cnt["T"]
            if getattr(self, "_tap_out", 0):
                self.wait("vector", "OUT", self._tap_out)
            for h in range(2):
                ah = 2 * p + h
                bank = 4 + ah % 3
                self.wait("vector", "T", t_avh[h])
                if h == 0:
                    self.wait("vector", "SC",
                              oacc_free[msl] if first else s_exp[p])
                if first:
                    avh_rel[ah] = self.op(
                        "vector", lambda e, msl=msl, bank=bank, h=h:
                        e.tensor_copy(
                            out=self.oacc[:, msl, h * 512:(h + 1) * 512],
                            in_=ps[:, bank * 512:(bank + 1) * 512]),
                        "V")
                else:
                    avh_rel[ah] = self.op(
                        "vector", lambda e, msl=msl, bank=bank, h=h, p4=p4:
                        e.scalar_tensor_tensor(
                            out=self.oacc[:, msl, h * 512:(h + 1) * 512],
                            in0=self.oacc[:, msl, h * 512:(h + 1) * 512],
                            scalar=st[:, ALPHA_C + p4:ALPHA_C + p4 + 1],
                            in1=ps[:, bank * 512:(bank + 1) * 512],
                            op0=Alu.mult, op1=Alu.add), "V")
            v_oacc[m] = self.cnt["V"]
            if self.debug and p == getattr(self, "dbg_pair", -1):
                ds = getattr(self, "dbg_slot", 0)
                self.wait("gpsimd", "V", self.cnt["V"])
                self.wait("gpsimd", "SC", self.cnt["SC"])
                self.wait("gpsimd", "OUT", self.cnt["OUT"])
                self.dma("gpsimd", self.dbg_x1[:, 0:2, :].rearrange(
                    "p a f -> p (a f)").rearrange("p (a f) -> p a f", a=2),
                    self.oacc[:, 0:2, :], "OUT")
                self._tap_out = self.cnt["OUT"]
                self.wait("gpsimd", "OUT", self.cnt["OUT"])
                self.op("gpsimd", lambda e, ds=ds: e.dma_start(
                    out=self.dbg_st[:, 0:1], in_=st[:, ds:ds + 1],
                    allow_slow_non_contiguous=True), "OUT", 16)
                self.wait("gpsimd", "OUT", self.cnt["OUT"])
                self.op("gpsimd", lambda e, ds=ds: e.dma_start(
                    out=self.dbg_st[:, 1:2], in_=st[:, L_C + ds:L_C + ds + 1],
                    allow_slow_non_contiguous=True), "OUT", 16)
                self._tap_out = self.cnt["OUT"]
            if diag:
                fin_q.append(m)

        for k0 in range(2):
            issue_load(k0)
        issued[0] = 2
        for i, (sw, j, ms) in enumerate(items):
                while pend and pend[0]["i"] <= i - 2:
                    emit_tail(pend.pop(0))
                buf = i % 2
                if i + 2 < n_items and issued[0] <= i + 2:
                    issue_load(i + 2)
                    issued[0] = i + 3
                for idx, m in enumerate(ms):
                    mi = m % 2
                    p2 = pair % 2
                    p4 = pair % 4
                    diag = (j == m // 2)
                    first = (j == 0)
                    sbank = 2 * p2
                    assert ld_k[i] > 0, (i, pair)
                    self.wait("tensor", f"KJ{buf}", ld_k[i])
                    if pair >= 2:
                        rsem, rval = bank_rel[pair - 2]
                        self.wait("tensor", rsem, rval)
                    if pair == 0:
                        self.wait("tensor", "V", self.qf_done)
                    for dk in range(8):
                        for kh in range(2):
                            self.op("tensor",
                                    lambda e, dk=dk, kh=kh, m=m, buf=buf, sbank=sbank:
                                    e.matmul(
                                        ps[:, (sbank + kh) * 512:(sbank + kh + 1) * 512],
                                        self.qfm[:, dk, m * 128:(m + 1) * 128],
                                        self.kj[buf][:, dk, kh * 512:(kh + 1) * 512],
                                        start=(dk == 0), stop=(dk == 7)),
                                    "T" if (dk == 7 and kh == 1) else None)
                    t_sc = self.cnt["T"]
                    if idx == len(ms) - 1:
                        t_sc_done[i] = t_sc
                    self.wait("vector", "T", t_sc)
                    if getattr(self, "_tap_out", 0):
                        self.wait("vector", "OUT", self._tap_out)
                    if not first:
                        if pair >= 4:
                            self.wait("vector", "SC", s_exp[pair - 4])
                        self.wait("vector", "SC", s_exp_m.get(m, 0))
                    sc_src = ps[:, sbank * 512:(sbank + 2) * 512]
                    if diag:
                        self.wait("vector", "SC", smsk_busy[mi])
                        self.wait("vector", "SM", dm_ld)
                        mk = self.op("vector", lambda e, sbank=sbank, mi=mi:
                                     e.tensor_add(out=self.smsk[mi][:],
                                                  in0=ps[:, sbank * 512:(sbank + 2) * 512],
                                                  in1=self.dmask[:, mi, :]), "V")
                        bank_rel[pair] = ("V", mk)
                        sc_src = self.smsk[mi][:]
                    if first:
                        vmark = self.op(
                            "vector", lambda e, m=m, sc_src=sc_src:
                            e.reduce_max(out=st[:, NEGM_C + m:NEGM_C + m + 1],
                                         in_=sc_src,
                                         axis=AX.X, negate=True), "V")
                    else:
                        self.op("vector", lambda e, p4=p4, sc_src=sc_src:
                                e.reduce_max(out=st[:, NMT_C + p4:NMT_C + p4 + 1],
                                             in_=sc_src,
                                             axis=AX.X, negate=True), "V")
                        self.op("vector", lambda e, p4=p4, m=m:
                                e.tensor_scalar(out=st[:, AD_C + p4:AD_C + p4 + 1],
                                                in0=st[:, NMT_C + p4:NMT_C + p4 + 1],
                                                scalar1=st[:, NEGM_C + m:NEGM_C + m + 1],
                                                scalar2=0.0,
                                                op0=Alu.subtract, op1=Alu.min), "V")
                        vmark = self.op(
                            "vector", lambda e, p4=p4, m=m:
                            e.tensor_scalar(out=st[:, NEGM_C + m:NEGM_C + m + 1],
                                            in0=st[:, AD_C + p4:AD_C + p4 + 1],
                                            scalar1=st[:, NEGM_C + m:NEGM_C + m + 1],
                                            scalar2=None,
                                            op0=Alu.add), "V")
                    self.wait("scalar", "V", vmark)
                    if pair >= 4:
                        self.wait("scalar", "T", t_tr[pair - 4])
                    if not first:
                        self.op("scalar", lambda e, p4=p4:
                                e.activation(out=st[:, ALPHA_C + p4:ALPHA_C + p4 + 1],
                                             in_=st[:, AD_C + p4:AD_C + p4 + 1],
                                             func=Act.Exp), "SC")
                    bias_c = NEGM_C + m
                    acc = st[:, L_C + m:L_C + m + 1] if first else \
                        st[:, RS_C + p4:RS_C + p4 + 1]
                    s_exp[pair] = self.op(
                        "scalar", lambda e, p4=p4, sc_src=sc_src, bias_c=bias_c, acc=acc:
                        e.activation(out=self.psb[p4][:],
                                     in_=sc_src,
                                     func=Act.Exp,
                                     bias=st[:, bias_c:bias_c + 1],
                                     accum_out=acc), "SC")
                    if diag:
                        smsk_busy[mi] = s_exp[pair]
                    else:
                        bank_rel[pair] = ("SC", s_exp[pair])
                    s_exp_m[m] = s_exp[pair]
                    if not first:
                        self.wait("vector", "SC", s_exp[pair])
                        v_lup[pair] = self.op(
                            "vector", lambda e, m=m, p4=p4:
                            e.scalar_tensor_tensor(
                                out=st[:, L_C + m:L_C + m + 1],
                                in0=st[:, L_C + m:L_C + m + 1],
                                scalar=st[:, ALPHA_C + p4:ALPHA_C + p4 + 1],
                                in1=st[:, RS_C + p4:RS_C + p4 + 1],
                                op0=Alu.mult, op1=Alu.add), "V")
                    if len(pend) == PEND:
                        emit_tail(pend.pop(0))
                    pend.append(dict(pair=pair, m=m, mi=mi, p2=p2, p4=p4,
                                     buf=buf, i=i, first=first, diag=diag,
                                     msl=m % 8, last=(idx == len(ms) - 1)))
                    pair += 1
                    if idx == 0 and 2 * j in ms:
                        while fin_q:
                            finalize(fin_q.pop(0))
                        for md in (2 * j, 2 * j + 1):
                            if md >= 2:
                                self.wait_lazy("scalar", "V", v_xq_used_slot,
                                               md - 2)
                            gp_xq[md] = self.dma("scalar",
                                                 self.xqst[md % 2][:],
                                                 self.xqtok[md],
                                                 f"XQ{md % 2}")
                    if fin_q and pair - last_fin[0] >= 2:
                        last_fin[0] = pair
                        finalize(fin_q.pop(0))
        while pend:
            emit_tail(pend.pop(0))
        while fin_q:
            finalize(fin_q.pop(0))
        self.x1ready = x1ready
        if self.debug:
            self.wait("gpsimd", "T", self.cnt["T"])
            self.wait("gpsimd", "V", self.cnt["V"])
            self.wait("gpsimd", "OUT", self.cnt["OUT"])
            self.dma("gpsimd", self.dbg_qfm, self.qfm[:], "OUT")

    def phase3(self, p2_marks):
        st, ps = self.st, self.ps
        for eng in ("sync", "gpsimd", "tensor", "vector", "scalar"):
            self.barrier_waits(eng, p2_marks)

        from concourse.hw_specs import get_activation_tables
        tabs = list(get_activation_tables(self.nc.m.arch))
        def load_tab_f(idx):
            def f(e, idx=idx):
                return e.add_instruction(mybir.InstLoadActFuncSet(
                    name=self.nc.get_next_instruction_name(),
                    act_func_set_id=idx, ins=[], outs=[]))
            return f
        silu_i = tabs.index("silu_and_others")
        gelu_i = tabs.index("gelu_and_others")
        nle_i = tabs.index("natural_log_exp_and_others")

        gp_out = {}
        wguard = [0, 0]
        prev_s4_vadd = 0
        for t in range(2):
            sb = 8 * t
            v_tfe = [0] * 8
            for lm in range(8):
                m = sb + lm
                if lm >= 1:
                    self.wait("tensor", "V", v_tfe[lm - 1])
                if t == 1 and lm == 0:
                    self.wait("tensor", "OUT", gp_out[(0, 0, 3)])
                    self.wait("tensor", "V", prev_s4_vadd)
                for dd in range(8):
                    self.op("tensor", lambda e, m=m, dd=dd:
                            e.transpose(self.psbf[:, dd * 128:(dd + 1) * 128],
                                        self.x1[:, m, dd * 128:(dd + 1) * 128],
                                        self.idb[:]),
                            "T" if dd == 7 else None)
                tmark = self.cnt["T"]
                self.wait("vector", "T", tmark)
                v_tfe[lm] = self.op(
                    "vector", lambda e, lm=lm:
                    e.tensor_copy(
                        out=self.x1fm[:, :, lm * 128:(lm + 1) * 128],
                        in_=self.psbf.rearrange("p (a f) -> p a f", a=8)),
                    "V")
            self.op("scalar", load_tab_f(silu_i))
            v_wcast = [0] * 32
            t_mm1 = {}
            s_silu = {}
            v_astt = {}
            for c in range(32):
                if c >= 1:
                    self.wait("sync", "V", v_wcast[c - 1])
                else:
                    self.wait("sync", "V", wguard[0])
                    self.wait("sync", "T", wguard[1])
                cl = self.dma("sync", self.wc[:], self.wblk[c], "WC")
                self.wait("vector", "WC", cl)
                if c >= 2:
                    self.wait("vector", "T", t_mm1[(c - 2, 1)])
                elif wguard[1]:
                    self.wait("vector", "T", wguard[1])
                v_wcast[c] = self.op(
                    "vector", lambda e, c=c:
                    e.tensor_copy(out=self.wcb[c % 2][:], in_=self.wc[:]), "V")
                for th in range(2):
                    idx = c * 2 + th
                    b0 = (idx % 3) * 2
                    self.wait("tensor", "V", v_wcast[c])
                    if c == 0 and th == 0:
                        self.wait("tensor", "V", v_tfe[7])
                    if idx >= 3:
                        self.wait("tensor", "SC", s_silu[idx - 3])
                        self.wait("tensor", "V", v_astt[idx - 3])
                    for w in range(2):
                        for dd in range(8):
                            bank = b0 + w
                            self.op("tensor",
                                    lambda e, c=c, th=th, w=w, dd=dd, bank=bank:
                                    e.matmul(
                                        ps[:, bank * 512:(bank + 1) * 512],
                                        self.wcb[c % 2][:, w, dd, :],
                                        self.x1fm[:, dd, th * 512:(th + 1) * 512],
                                        start=(dd == 0), stop=(dd == 7)),
                                    "T" if (w == 1 and dd == 7) else None)
                    t_mm1[(c, th)] = self.cnt["T"]
                    self.wait("scalar", "T", t_mm1[(c, th)])
                    if idx >= 2:
                        self.wait("scalar", "V", v_astt[idx - 2])
                    s_silu[idx] = self.op(
                        "scalar", lambda e, c=c, idx=idx, b0=b0:
                        e.activation(out=self.s1t[idx % 2][:],
                                     in_=ps[:, b0 * 512:(b0 + 1) * 512],
                                     func=Act.Silu,
                                     scale=self.s1s[:, c:c + 1]), "SC")
                    self.wait("vector", "SC", s_silu[idx])
                    v_astt[idx] = self.op(
                        "vector", lambda e, c=c, th=th, idx=idx, b0=b0:
                        e.scalar_tensor_tensor(
                            out=self.afm[:, c, th * 512:(th + 1) * 512],
                            in0=ps[:, (b0 + 1) * 512:(b0 + 2) * 512],
                            scalar=self.s2s[:, c:c + 1],
                            in1=self.s1t[idx % 2][:],
                            op0=Alu.mult, op1=Alu.mult), "V")
            self.op("scalar", load_tab_f(gelu_i))
            if t == 1:
                self.wait("scalar", "OUT", gp_out[(0, 1, 3)])
            wguard = [v_wcast[31], t_mm1[(31, 1)]]
            for stage, (wt, src, dst, scl) in enumerate(
                    ((self.h1t, self.afm, self.bfm, self.sh1s),
                     (self.h2t, self.bfm, self.afm, self.sh2s))):
                v_hcast = {}
                t_hmm = {}
                s_gel = {}
                for sc in range(8):
                    for dd in range(32):
                        gi = sc * 32 + dd
                        slot = gi % 2
                        if gi >= 2:
                            self.wait("sync", "V", v_hcast[gi - 2])
                        else:
                            self.wait("sync", "V", wguard[0])
                            self.wait("sync", "T", wguard[1])
                        cl = self.dma("sync", self.hbuf[slot][:],
                                      wt[sc, dd], f"HB{slot}")
                        self.wait("vector", f"HB{slot}", cl)
                        if gi >= 2:
                            self.wait("vector", "T", t_hmm[gi - 2])
                        else:
                            self.wait("vector", "T", wguard[1])
                        v_hcast[gi] = self.op(
                            "vector", lambda e, slot=slot:
                            e.tensor_copy(out=self.hbufb[slot][:],
                                          in_=self.hbuf[slot][:]), "V")
                        self.wait("tensor", "V", v_hcast[gi])
                        if dd == 0 and sc == 0:
                            if stage == 0:
                                self.wait("tensor", "SC", s_silu[63])
                                self.wait("tensor", "V", v_astt[63])
                            else:
                                for c4 in range(4):
                                    self.wait("tensor", "SC", s_gel_prev[(7, c4)])
                        for c4 in range(4):
                            if dd == 0 and sc >= 1:
                                self.wait("tensor", "SC", s_gel[(sc - 1, c4)])
                            for th in range(2):
                                bank = c4 * 2 + th
                                self.op("tensor",
                                        lambda e, slot=slot, c4=c4, th=th, dd=dd,
                                        bank=bank, src=src:
                                        e.matmul(
                                            ps[:, bank * 512:(bank + 1) * 512],
                                            self.hbufb[slot][:, c4 * 128:(c4 + 1) * 128],
                                            src[:, dd, th * 512:(th + 1) * 512],
                                            start=(dd == 0), stop=(dd == 31)),
                                        "T" if (c4 == 3 and th == 1) else None)
                        t_hmm[gi] = self.cnt["T"]
                    for c4 in range(4):
                        self.wait("scalar", "T", t_hmm[sc * 32 + 31])
                        s_gel[(sc, c4)] = self.op(
                            "scalar", lambda e, sc=sc, c4=c4, dst=dst, scl=scl:
                            e.activation(out=dst[:, sc * 4 + c4, :],
                                         in_=ps[:, c4 * 1024:(c4 + 1) * 1024],
                                         func=Act.Gelu,
                                         scale=scl[:, sc * 4 + c4:sc * 4 + c4 + 1]),
                            "SC")
                s_gel_prev = s_gel
                wguard = [v_hcast[255], t_hmm[255]]
            self.op("scalar", load_tab_f(nle_i))
            v_add = {}
            v_w3cast = {}
            t_w3mm = {}
            for h in range(2):
                for ddb in range(4):
                    for dl in range(8):
                        dd = ddb * 8 + dl
                        gi = h * 32 + dd
                        slot = gi % 2
                        if gi >= 2:
                            self.wait("sync", "V", v_w3cast[gi - 2])
                        else:
                            self.wait("sync", "V", wguard[0])
                            self.wait("sync", "T", wguard[1])
                        cl = self.dma("sync", self.w3b[slot][:],
                                      self.w3t[dd], f"W3{slot}")
                        self.wait("vector", f"W3{slot}", cl)
                        if gi >= 2:
                            self.wait("vector", "T", t_w3mm[gi - 2])
                        else:
                            self.wait("vector", "T", wguard[1])
                        v_w3cast[gi] = self.op(
                            "vector", lambda e, slot=slot:
                            e.tensor_copy(out=self.w3bb[slot][:],
                                          in_=self.w3b[slot][:]), "V")
                        self.wait("tensor", "V", v_w3cast[gi])
                        if dl == 0:
                            if (h, ddb) == (0, 0):
                                for c4 in range(4):
                                    self.wait("tensor", "SC", s_gel_prev[(7, c4)])
                            else:
                                for tc in range(4):
                                    self.wait("tensor", "V",
                                              v_add[(h, ddb - 1, tc) if ddb > 0
                                                    else (h - 1, 3, tc)])
                        for tc in range(4):
                            for oh in range(2):
                                bank = tc * 2 + oh
                                self.op("tensor",
                                        lambda e, slot=slot, tc=tc, oh=oh, dd=dd,
                                        h=h, bank=bank, dl=dl:
                                        e.matmul(
                                            ps[:, bank * 512:(bank + 1) * 512],
                                            self.afm[:, dd, h * 512 + tc * 128:
                                                     h * 512 + (tc + 1) * 128],
                                            self.w3bb[slot][:, oh * 512:(oh + 1) * 512],
                                            start=(dl == 0), stop=(dl == 7)),
                                        "T" if (tc == 3 and oh == 1) else None)
                        t_w3mm[gi] = self.cnt["T"]
                    ob = self.osb[h]
                    for tc in range(4):
                        self.wait("vector", "T", t_w3mm[h * 32 + ddb * 8 + 7])
                        if ddb == 0:
                            if t == 1:
                                self.wait("vector", "OUT", gp_out[(0, h, 3)])
                            v_add[(h, ddb, tc)] = self.op(
                                "vector", lambda e, tc=tc, ob=ob:
                                e.tensor_copy(out=ob[:, tc, :],
                                              in_=ps[:, tc * 1024:(tc + 1) * 1024]),
                                "V")
                        else:
                            v_add[(h, ddb, tc)] = self.op(
                                "vector", lambda e, tc=tc, ob=ob:
                                e.tensor_add(out=ob[:, tc, :],
                                             in0=ob[:, tc, :],
                                             in1=ps[:, tc * 1024:(tc + 1) * 1024]),
                                "V")
                ob = self.osb[h]
                for tc in range(4):
                    m = sb + h * 4 + tc
                    par = tc % 2
                    self.op("vector", lambda e, tc=tc, ob=ob:
                            e.tensor_mul(out=ob[:, tc, :],
                                         in0=ob[:, tc, :],
                                         in1=self.s3b[:]), "V")
                    self.op("vector", lambda e, tc=tc, m=m, par=par, ob=ob:
                            e.scalar_tensor_tensor(out=self.ct2[par][:],
                                                   in0=self.x1[:, m, :],
                                                   scalar=1.0,
                                                   in1=ob[:, tc, :],
                                                   op0=Alu.mult, op1=Alu.add),
                            "V")
                    self.op("vector", lambda e, par=par:
                            e.reduce_sum(out=st[:, MUS_C + par:MUS_C + par + 1],
                                         in_=self.ct2[par][:], axis=AX.X), "V")
                    self.op("vector", lambda e, par=par:
                            e.tensor_scalar_mul(out=st[:, MU_C + par:MU_C + par + 1],
                                                in0=st[:, MUS_C + par:MUS_C + par + 1],
                                                scalar1=1.0 / 1024.0), "V")
                    self.op("vector", lambda e, par=par:
                            e.tensor_scalar_sub(out=self.ct2[par][:],
                                                in0=self.ct2[par][:],
                                                scalar1=st[:, MU_C + par:MU_C + par + 1]),
                            "V")
                    self.op("vector", lambda e, par=par, tc=tc, ob=ob:
                            e.scalar_tensor_tensor(out=ob[:, tc, :],
                                                   in0=self.ct2[par][:], scalar=1.0,
                                                   in1=self.ct2[par][:],
                                                   op0=Alu.mult, op1=Alu.mult,
                                                   accum_out=st[:, VS_C + par:VS_C + par + 1]),
                            "V")
                    vmark = self.op(
                        "vector", lambda e, par=par:
                        e.tensor_scalar(out=st[:, RV_C + par:RV_C + par + 1],
                                        in0=st[:, VS_C + par:VS_C + par + 1],
                                        scalar1=1.0 / 1024.0, scalar2=1e-5,
                                        op0=Alu.mult, op1=Alu.add), "V")
                    self.wait("scalar", "V", vmark)
                    self.op("scalar", lambda e, par=par:
                            e.activation(out=st[:, LNV_C + par:LNV_C + par + 1],
                                         in_=st[:, RV_C + par:RV_C + par + 1],
                                         func=Act.Ln), "SC")
                    smark = self.op(
                        "scalar", lambda e, par=par:
                        e.activation(out=st[:, RSTD_C + par:RSTD_C + par + 1],
                                     in_=st[:, LNV_C + par:LNV_C + par + 1],
                                     func=Act.Exp, scale=-0.5), "SC")
                    self.wait("vector", "SC", smark)
                    vmark = self.op(
                        "vector", lambda e, tc=tc, par=par, ob=ob:
                        e.tensor_scalar_mul(out=ob[:, tc, :],
                                            in0=self.ct2[par][:],
                                            scalar1=st[:, RSTD_C + par:RSTD_C + par + 1]),
                        "V")
                    self.wait("gpsimd", "V", vmark)
                    self.wait("gpsimd", "OUT", self.cnt["OUT"])
                    gp_out[(t, h, tc)] = self.dma(
                        "gpsimd", self.outd[m], ob[:, tc, :], "OUT")
            wguard = [v_w3cast[63], t_w3mm[63]]
            prev_s4_vadd = v_add[(1, 3, 3)]
        if self.debug:
            self.wait("gpsimd", "V", self.cnt["V"])
            self.wait("gpsimd", "SC", self.cnt["SC"])
            self.wait("gpsimd", "OUT", self.cnt["OUT"])
            if getattr(self, "dbg_pair", -1) < 0:
                self.dma("gpsimd", self.dbg_x1, self.x1[:], "OUT")
            self.wait("gpsimd", "OUT", self.cnt["OUT"])
            if getattr(self, "dbg_pair", -1) < 0:
                self.dma("gpsimd", self.dbg_st[:, 0:32], self.st[:, 0:32], "OUT")
            self.wait("gpsimd", "OUT", self.cnt["OUT"])
            m1 = self.dma("gpsimd", self.kj[0][:], self.kscr[0], "OUT")
            self.wait("gpsimd", "OUT", m1)
            self.dma("gpsimd", self.dbg_k0, self.kj[0][:], "OUT")
            self.wait("gpsimd", "OUT", self.cnt["OUT"])
            m2 = self.dma("gpsimd", self.vj[0][:], self.vscr[0], "OUT")
            self.wait("gpsimd", "OUT", m2)
            self.dma("gpsimd", self.dbg_v0, self.vj[0][:], "OUT")
        self.wait("gpsimd", "OUT", self.cnt["OUT"])

    def build(self):
        self.phase1()
        p1 = self.snap()
        self.phase2(p1)
        p2 = self.snap()
        self.phase3(p2)
        steps = self.steps
        with self.nc.Block() as block:
            @block.sync
            def _(e):
                for eng, fn in steps:
                    if eng == "sync":
                        fn(e)

            @block.gpsimd
            def _(e):
                for eng, fn in steps:
                    if eng == "gpsimd":
                        fn(e)

            @block.tensor
            def _(e):
                for eng, fn in steps:
                    if eng == "tensor":
                        fn(e)

            @block.vector
            def _(e):
                for eng, fn in steps:
                    if eng == "vector":
                        fn(e)

            @block.scalar
            def _(e):
                for eng, fn in steps:
                    if eng == "scalar":
                        fn(e)
        return self.nc


def _stage_inputs(inp):
    f32 = np.float32
    x = np.ascontiguousarray(inp["L0_attn_x"], dtype=f32)
    W = np.ascontiguousarray(inp["L0_attn_W"], dtype=f32)
    wkT = np.ascontiguousarray(W[:D].T)
    wvT = np.ascontiguousarray(W[D:].T)
    wk_l = np.ascontiguousarray(wkT.reshape(8, 128, 1024).transpose(1, 0, 2))
    wv_l = np.ascontiguousarray(wvT.reshape(8, 128, 1024).transpose(1, 0, 2))
    w1 = np.asarray(inp["L0_ffn_w1_w"], dtype=f32)
    w2 = np.asarray(inp["L0_ffn_w2_w"], dtype=f32)
    wblk = np.ascontiguousarray(
        np.stack([w1.reshape(32, 128, 8, 128), w2.reshape(32, 128, 8, 128)],
                 axis=1).transpose(0, 4, 1, 3, 2))
    h1 = np.asarray(inp["L0_ffn_h1_w"], dtype=f32)
    h2 = np.asarray(inp["L0_ffn_h2_w"], dtype=f32)
    h1t = np.ascontiguousarray(h1.reshape(8, 512, 32, 128).transpose(0, 2, 3, 1))
    h2t = np.ascontiguousarray(h2.reshape(8, 512, 32, 128).transpose(0, 2, 3, 1))
    w3 = np.asarray(inp["L0_ffn_w3_w"], dtype=f32)
    w3t = np.ascontiguousarray(w3.T.reshape(32, 128, 1024))
    s1l = np.ascontiguousarray(np.asarray(inp["L0_ffn_w1_s"], f32).reshape(32, 128).T)
    s2l = np.ascontiguousarray(np.asarray(inp["L0_ffn_w2_s"], f32).reshape(32, 128).T)
    sh1l = np.ascontiguousarray(np.asarray(inp["L0_ffn_h1_s"], f32).reshape(32, 128).T)
    sh2l = np.ascontiguousarray(np.asarray(inp["L0_ffn_h2_s"], f32).reshape(32, 128).T)
    s3bl = np.ascontiguousarray(
        np.broadcast_to(np.asarray(inp["L0_ffn_w3_s"], f32)[None, :], (128, 1024)))
    identf = np.eye(128, dtype=f32)

    in_maps = []
    for core in range(8):
        b, c = divmod(core, 4)
        xb = x[b]
        qrows = ((4 * np.arange(16) + c)[:, None] * 128 +
                 np.arange(128)[None, :]).reshape(-1)
        xq = np.ascontiguousarray(xb[qrows])
        xqfm = np.ascontiguousarray(
            xq.T.reshape(8, 128, 2048).transpose(1, 0, 2))
        xqtok = np.ascontiguousarray(xq.reshape(16, 128, 1024))
        xkv = np.ascontiguousarray(
            xb.reshape(16, 512, 8, 128).transpose(0, 3, 2, 1))
        i = np.arange(128)[:, None]
        kk = np.arange(1024)[None, :]
        dmsk = np.stack([
            np.where(kk <= v * 512 + c * 128 + i, np.float32(0.0),
                     np.float32(NEG_BIG))
            for v in range(2)]).astype(ml_dtypes.bfloat16)
        in_maps.append(dict(
            xkv=xkv, xqfm=xqfm, xqtok=xqtok, wkT=wk_l, wvT=wv_l,
            wblk=wblk, h1t=h1t, h2t=h2t, w3t=w3t,
            s1l=s1l, s2l=s2l, sh1l=sh1l, sh2l=sh2l, s3bl=s3bl,
            identf=identf, dmaskin=dmsk))
    return in_maps


_NC_CACHE = {}


def _get_nc():
    if "nc" not in _NC_CACHE:
        _NC_CACHE["nc"] = KB().build()
    return _NC_CACHE["nc"]


def kernel(**inputs):
    nc = _get_nc()
    in_maps = _stage_inputs(inputs)
    res = run_bass_kernel_spmd(nc, in_maps, core_ids=list(range(8)))
    out = np.zeros((2, S, D), dtype=np.float32)
    for core in range(8):
        b, c = divmod(core, 4)
        o = res.results[core]["out"].reshape(2048, 1024)
        qrows = ((4 * np.arange(16) + c)[:, None] * 128 +
                 np.arange(128)[None, :]).reshape(-1)
        out[b, qrows] = o
    return out
